# version 04
# baseline (speedup 1.0000x reference)
import contextlib
import numpy as np
import concourse.bass as bass
import concourse.mybir as mybir
from concourse.bass_utils import run_bass_kernel_spmd
from concourse.aot_env import allow_ici_all_to_all

F32 = mybir.dt.float32
BF16 = mybir.dt.bfloat16
AF = mybir.ActivationFunctionType
ALU = mybir.AluOpType

NCORES = 8
P = 128


class Cfg:
    def __init__(self, D, B, S, I, Tc, ZC, NC2, HU):
        self.D, self.B, self.S, self.I = D, B, S, I
        self.R, self.DST = 128, 16
        self.Tc, self.ZC, self.NC2, self.HU = Tc, ZC, NC2, HU
        self.DSL = D // NCORES
        self.KD = self.DSL // P
        self.TALL = B * S
        self.TOK = self.TALL // NCORES
        self.NCH = self.TALL // Tc
        self.NPB = S // Tc
        self.CPD = self.TOK // Tc
        self.MI = I // P
        self.DD = D // P
        self.FC = self.TOK // ZC
        self.FC2 = self.TOK // NC2
        self.HH = self.TOK // HU
        self.NMM = 512
        assert self.DSL % P == 0 and S % Tc == 0 and self.TOK % Tc == 0
        assert self.TOK % ZC == 0 and self.TOK % NC2 == 0 and self.TOK % HU == 0
        assert I % P == 0 and D % P == 0 and ZC <= 512 and NC2 <= 512 and HU <= 1024


def full_cfg():
    return Cfg(D=4096, B=2, S=8192, I=11008, Tc=1024, ZC=128, NC2=512, HU=1024)


def _cdiv(a, b):
    return (a + b - 1) // b


def build_nc(cfg):
    c = cfg
    nc = bass.Bass("TRN2", target_bir_lowering=False, debug=False)

    x_sl = nc.dram_tensor("x_sl", [c.DSL, c.TALL], F32, kind="ExternalInput").ap()
    xproj = nc.dram_tensor("xproj", [c.DSL, 160], F32, kind="ExternalInput").ap()
    dtw = nc.dram_tensor("dtw", [c.R, c.DSL], F32, kind="ExternalInput").ap()
    a_sl = nc.dram_tensor("a_sl", [P, c.KD, c.DST], F32, kind="ExternalInput").ap()
    dtb = nc.dram_tensor("dtb", [P, c.KD], F32, kind="ExternalInput").ap()
    dsl = nc.dram_tensor("dsl", [P, c.KD], F32, kind="ExternalInput").ap()
    w1tt = nc.dram_tensor("w1tt", [c.MI, P, c.DD, P], F32, kind="ExternalInput").ap()
    w3tt = nc.dram_tensor("w3tt", [c.MI, P, c.DD, P], F32, kind="ExternalInput").ap()
    w2tt = nc.dram_tensor("w2tt", [c.DD, P, c.MI * P], F32, kind="ExternalInput").ap()
    w1sp = nc.dram_tensor("w1sp", [P, c.MI], F32, kind="ExternalInput").ap()
    w3sp = nc.dram_tensor("w3sp", [P, c.MI], F32, kind="ExternalInput").ap()
    w2sp = nc.dram_tensor("w2sp", [P, c.DD], F32, kind="ExternalInput").ap()
    out_e = nc.dram_tensor("out", [c.D, c.TOK], F32, kind="ExternalOutput").ap()

    xp_part = nc.dram_tensor("xp_part", [c.NCH, 160, c.Tc], F32).ap()
    xp_red = nc.dram_tensor("xp_red", [c.NCH, 160, c.Tc], F32).ap()
    a2a_in = nc.dram_tensor("a2a_in", [c.D, c.TOK], F32).ap()
    a2a_out = nc.dram_tensor("a2a_out", [c.D, c.TOK], F32).ap()
    h_dram = nc.dram_tensor("h_dram", [P, c.MI, c.TOK], BF16).ap()
    x1_dram = nc.dram_tensor("x1_dram", [P, c.DD, c.TOK], BF16).ap()
    z2_dram = nc.dram_tensor("z2_dram", [P, c.DD, c.NC2], F32).ap()
    w2bf_d = nc.dram_tensor("w2bf_d", [c.DD, P, c.MI * P], BF16).ap()
    z2_dram2 = nc.dram_tensor("z2_dram2", [P, c.DD, c.NC2], F32).ap()

    stack = contextlib.ExitStack()
    nc._cleanup_stack = stack

    arena = None
    CAP = None
    for ai, cand in enumerate(range(nc.SBUF_PARTITION_SIZE_BYTES, 150000, -2048)):
        try:
            arena = stack.enter_context(nc.sbuf_tensor(f"arena{ai}", [P, cand // 4], F32))
            CAP = cand
            break
        except (AssertionError, ValueError):
            continue
    assert arena is not None

    class Region:
        def __init__(self, base):
            self.off = base

        def alloc(self, nbytes):
            nbytes = (nbytes + 31) & ~31
            o = self.off
            self.off += nbytes
            assert self.off <= CAP, f"arena overflow {self.off} > {CAP}"
            return o

    def carve(off, shape, dtype=F32, p0=0, pn=P):
        esz = 4 if dtype == F32 else 2
        n = 1
        for s in shape:
            n *= s
        words = (n * esz + 3) // 4
        ap = arena[p0:p0 + pn, off // 4: off // 4 + words]
        if dtype != F32:
            ap = ap.bitcast(dtype)[:, :n]
        if len(shape) > 1:
            letters = "abcdefg"[: len(shape)]
            spec = "p (" + " ".join(letters) + ") -> p " + " ".join(letters)
            kw = {letters[i]: shape[i] for i in range(len(shape))}
            ap = ap.rearrange(spec, **kw)
        return ap

    RW = max(c.ZC, c.NC2)
    per = Region(0)
    t_state = carve(per.alloc(c.KD * c.DST * 4), [c.KD, c.DST])
    t_araw = carve(per.alloc(c.KD * c.DST * 4), [c.KD, c.DST])
    t_aneg = carve(per.alloc(c.KD * c.DST * 4), [c.KD, c.DST])
    t_ta = carve(per.alloc(c.KD * c.DST * 4), [c.KD, c.DST])
    t_dtb = carve(per.alloc(c.KD * 4), [c.KD])
    t_dsl = carve(per.alloc(c.KD * 4), [c.KD])
    t_d1p = carve(per.alloc(c.KD * 4), [c.KD])
    t_xpw = carve(per.alloc(c.KD * 160 * 4), [c.KD, 160])
    t_dtw = carve(per.alloc(c.DSL * 4), [c.DSL])
    t_w1sc = carve(per.alloc(c.MI * 4), [c.MI])
    t_w3sc = carve(per.alloc(c.MI * 4), [c.MI])
    t_w2sc = carve(per.alloc(c.DD * 4), [c.DD])
    t_ones_col = carve(per.alloc(4), [1])
    t_ones_row = carve(per.alloc(P * 4), [P], p0=0, pn=1)
    t_cns = carve(per.alloc(4 * 4), [4])
    t_mrow = carve(per.alloc(RW * 4), [RW], p0=0, pn=1)
    t_vrow = carve(per.alloc(RW * 4), [RW], p0=0, pn=1)
    t_rrow = carve(per.alloc(RW * 4), [RW], p0=0, pn=1)
    PER_END = per.off

    CB = 1024
    p1 = Region(PER_END)
    t_xbuf = carve(p1.alloc(2 * c.KD * c.Tc * 4), [2, c.KD, c.Tc])
    t_dtin = carve(p1.alloc(2 * c.Tc * 4), [2, c.Tc])
    t_dtins = carve(p1.alloc(c.Tc * 4), [c.Tc])
    t_bcs = carve(p1.alloc(c.Tc * 4), [c.Tc], p0=0, pn=32)
    t_dt = carve(p1.alloc(c.KD * c.Tc * 4), [c.KD, c.Tc])
    t_u = carve(p1.alloc(c.KD * c.Tc * 4), [c.KD, c.Tc])
    t_y = carve(p1.alloc(c.KD * c.Tc * 4), [c.KD, c.Tc])
    t_t1 = carve(p1.alloc(c.KD * c.Tc * 4), [c.KD, c.Tc])
    t_t3 = carve(p1.alloc(c.KD * c.Tc * 4), [c.KD, c.Tc])
    t_t02 = carve(p1.alloc(c.Tc * 4), [c.Tc])
    t_dA = carve(p1.alloc(2 * c.Tc * 4), [2, c.Tc])
    t_dbx = carve(p1.alloc(c.Tc * 4), [c.Tc])
    t_h = carve(p1.alloc(c.Tc * 4), [c.Tc])
    t_tmp = carve(p1.alloc(c.Tc * 4), [c.Tc])
    t_brep = carve(p1.alloc(2 * c.Tc * 4), [2, c.Tc])
    t_crep = carve(p1.alloc(2 * c.Tc * 4), [2, c.Tc])
    t_z = carve(p1.alloc(2 * c.Tc * 4), [2, c.Tc])
    t_cst = carve(p1.alloc(2 * CB * 4), [2, CB])
    t_cbf = carve(p1.alloc(2 * CB * 2), [2, CB], BF16)
    assert p1.off <= CAP

    f1 = Region(PER_END)
    t_x1 = carve(f1.alloc(c.DD * c.TOK * 2), [c.DD, c.TOK], BF16)
    X1_END = f1.off
    t_zfull = carve(f1.alloc(2 * c.DD * c.ZC * 4), [2, c.DD, c.ZC])
    t_zsqp = carve(f1.alloc(4 * c.ZC * 4), [4, c.ZC])
    t_tmpz = carve(f1.alloc(c.ZC * 4), [c.ZC])
    assert f1.off <= CAP
    f2a = Region(X1_END)
    t_wst = carve(f2a.alloc(c.DD * P * 4), [c.DD * P])
    t_w1bf = carve(f2a.alloc(2 * c.DD * P * 2), [2, c.DD, P], BF16)
    t_w3bf = carve(f2a.alloc(2 * c.DD * P * 2), [2, c.DD, P], BF16)
    t_silu = carve(f2a.alloc(2 * c.HU * 4), [2, c.HU])
    t_hbuf = carve(f2a.alloc(2 * c.HU * 2), [2, c.HU], BF16)
    assert f2a.off <= CAP
    f2b = Region(PER_END)
    t_hbig = carve(f2b.alloc(c.MI * c.NC2 * 2), [c.MI, c.NC2], BF16)
    t_w2bf = carve(f2b.alloc(2 * c.MI * P * 2), [2, c.MI, P], BF16)
    t_x1c = carve(f2b.alloc(c.DD * c.NC2 * 2), [c.DD, c.NC2], BF16)
    t_z2z = carve(f2b.alloc(2 * 2 * c.NC2 * 4), [2, 2, c.NC2])
    t_z2l = carve(f2b.alloc(2 * c.NC2 * 4), [2, c.NC2])
    t_outb = carve(f2b.alloc(2 * c.NC2 * 4), [2, c.NC2])
    t_tmp2 = carve(f2b.alloc(c.NC2 * 4), [c.NC2])
    assert f2b.off <= CAP

    ps = stack.enter_context(nc.psum_tensor("ps", [P, 4096], F32))
    BK = 512
    ps_xp1 = ps[:, 0:c.Tc]
    ps_xp2 = ps[0:32, 2 * BK:2 * BK + c.Tc]
    ps_dt = [ps[:, 4 * BK + 2 * i * BK: 4 * BK + 2 * i * BK + c.Tc] for i in range(2)]
    ps_stat_z = ps[0:1, 0:c.ZC]
    ps_stat_q = ps[0:1, BK:BK + c.ZC]
    ps_m1 = ps[:, 2 * BK:2 * BK + c.ZC]
    ps_r1 = ps[:, 3 * BK:3 * BK + c.ZC]
    ps_up = [ps[:, 0:c.HU], ps[:, 2 * BK:2 * BK + c.HU]]
    ps_gt = [ps[:, 4 * BK:4 * BK + c.HU], ps[:, 6 * BK:6 * BK + c.HU]]
    ps_o2 = [ps[:, 0:c.NC2], ps[:, BK:BK + c.NC2]]
    ps_st2_z = ps[0:1, 2 * BK:2 * BK + c.NC2]
    ps_st2_q = ps[0:1, 3 * BK:3 * BK + c.NC2]
    ps_m2 = ps[:, 4 * BK:4 * BK + c.NC2]
    ps_r2 = ps[:, 5 * BK:5 * BK + c.NC2]

    semnames = [
        "s_x0", "s_x1b", "s_bc0", "s_bc1", "s_zd0", "s_zd1",
        "s_xp", "s_dt", "s_xpe", "s_sp", "s_dts",
        "s_da", "s_hts", "s_vbc", "s_z", "s_gw", "s_gr0", "s_gr1", "s_cc", "s_gc",
        "s_prc", "s_vpr",
        "s_wcl0", "s_wcl1", "s_wcv", "s_wcs0", "s_wcs1",
        "s_zl0", "s_zl1", "s_stat", "s_str", "s_st", "s_rep1", "s_x1", "s_x1d",
        "s_wl", "s_cv", "s_ffn1", "s_sil", "s_h", "s_hd0", "s_hd1",
        "s_hla", "s_hlb", "s_x1l", "s_w2a", "s_w2b", "s_ffn2", "s_z2", "s_sq1", "s_sq2",
        "s_stat2", "s_str2", "s_st2", "s_rep2", "s_z2d0", "s_z2d1",
        "s_z2l0", "s_z2l1", "s_o", "s_out0", "s_out1", "s_vch", "s_sch",
    ]
    S = {}
    for nm in semnames:
        S[nm] = stack.enter_context(nc.semaphore(nm))
    XS = ["s_x0", "s_x1b"]

    DST, KD, Tc, NCH = c.DST, c.KD, c.Tc, c.NCH
    NSUB = min(c.NMM, Tc)
    NT = Tc // NSUB
    HH = c.HH
    NHG = min(4, c.MI)
    mper = _cdiv(c.MI, NHG)
    NGR = _cdiv(c.MI, mper)
    NGC = 8

    conv_blocks = []
    nb2 = _cdiv(c.MI * P, CB)
    for do in range(c.DD):
        for q in range(nb2):
            lo, hi = q * CB, min((q + 1) * CB, c.MI * P)
            conv_blocks.append((w2tt[do][:, lo:hi], w2bf_d[do][:, lo:hi], hi - lo))
    NBLK = len(conv_blocks)
    NSLOT = NCH * DST
    base, extra = divmod(NBLK, NSLOT)
    slot_lo = [0]
    for s_ in range(NSLOT):
        slot_lo.append(slot_lo[-1] + base + (1 if s_ < extra else 0))

    x_r = x_sl.rearrange("(k p) t -> p k t", p=P)

    def W(eng, sem, v):
        if v > 0:
            eng.wait_ge(S[sem], v)

    class _Proxy:
        def __init__(self, ch, ins):
            self._ch, self._ins = ch, ins

        def then_inc(self, sem, val=1):
            ch = self._ch
            assert ch._pending is self._ins
            n = ch._counts.get(sem.name, 0) + val
            ch._counts[sem.name] = n
            self._ins.then_inc(sem, val)
            ch._last = (sem, n)
            ch._pending = None
            return self

    class _Chain:

        def __init__(self, e, sem):
            self._e, self._sem = e, sem
            self._counts = {}
            self._pending = None
            self._last = None

        def _flush(self):
            if self._pending is not None:
                n = self._counts.get(self._sem.name, 0) + 1
                self._counts[self._sem.name] = n
                self._pending.then_inc(self._sem, 1)
                self._last = (self._sem, n)
                self._pending = None

        def wait_ge(self, sem, v):
            self._flush()
            self._e.wait_ge(sem, v)

        def __getattr__(self, name):
            f = getattr(self._e, name)
            ch = self

            def wrapped(*a, **k):
                ch._flush()
                if ch._last is not None:
                    ch._e.wait_ge(ch._last[0], ch._last[1])
                ins = f(*a, **k)
                ch._pending = ins
                return _Proxy(ch, ins)
            return wrapped

    blk = stack.enter_context(nc.Block())

    @blk.sync
    def _(eng):
        for i in range(min(2, NCH)):
            eng.dma_start(out=t_xbuf[:, i % 2], in_=x_r[:, :, i * Tc:(i + 1) * Tc]).then_inc(S[XS[i % 2]], 16)
        for cc in range(NCH):
            for n in range(DST):
                flat = cc * DST + n
                W(eng, "s_vbc", flat - 1)
                W(eng, "s_cc", cc + 1)
                src_b = xp_red[cc, c.R + n: c.R + n + 1, :].broadcast_to([P, Tc])
                src_c = xp_red[cc, c.R + DST + n: c.R + DST + n + 1, :].broadcast_to([P, Tc])
                eng.dma_start(out=t_brep[:, n % 2], in_=src_b).then_inc(S["s_bc0" if n % 2 == 0 else "s_bc1"], 16)
                eng.dma_start(out=t_crep[:, n % 2], in_=src_c).then_inc(S["s_bc0" if n % 2 == 0 else "s_bc1"], 16)
                for b in range(slot_lo[flat], slot_lo[flat + 1]):
                    W(eng, "s_wcv", b - 1)
                    srcap, dstap, w_ = conv_blocks[b]
                    eng.dma_start(out=t_cst[:, b % 2][:, 0:w_],
                                  in_=srcap).then_inc(S["s_wcl0" if b % 2 == 0 else "s_wcl1"], 16)
            j = cc // c.CPD
            coff = (cc % c.CPD) * Tc
            for kk in range(KD):
                q = cc * KD + kk
                W(eng, "s_z", q + 1)
                r0 = j * c.DSL + kk * P
                eng.dma_start(out=a2a_in[r0:r0 + P, coff:coff + Tc],
                              in_=t_z[:, q % 2]).then_inc(S["s_zd0" if q % 2 == 0 else "s_zd1"], 16)
            if cc + 2 < NCH:
                W(eng, "s_z", KD * (cc + 1))
                i = cc + 2
                eng.dma_start(out=t_xbuf[:, i % 2], in_=x_r[:, :, i * Tc:(i + 1) * Tc]).then_inc(S[XS[i % 2]], 16)

        a2o = a2a_out.rearrange("(dd p) t -> p dd t", p=P)
        W(eng, "s_cc", NCH + 1)
        for fc in range(c.FC):
            W(eng, "s_x1", (fc - 1) * c.DD)
            hd = c.DD // 2
            for g in range(2):
                eng.dma_start(out=t_zfull[:, fc % 2, g * hd:(g + 1) * hd],
                              in_=a2o[:, g * hd:(g + 1) * hd, fc * c.ZC:(fc + 1) * c.ZC]).then_inc(
                                  S["s_zl0" if fc % 2 == 0 else "s_zl1"], 16)
        for fc in range(c.FC):
            for dd in range(c.DD):
                W(eng, "s_x1", fc * c.DD + dd + 1)
                eng.dma_start(out=x1_dram[:, dd, fc * c.ZC:(fc + 1) * c.ZC],
                              in_=t_x1[:, dd, fc * c.ZC:(fc + 1) * c.ZC]).then_inc(S["s_x1d"], 16)
        W(eng, "s_x1", c.FC * c.DD)
        for m in range(c.MI):
            W(eng, "s_cv", 2 * m)
            eng.dma_start(out=t_wst[:, :], in_=w1tt[m].rearrange("p k m -> p (k m)")).then_inc(S["s_wl"], 16)
            W(eng, "s_cv", 2 * m + 1)
            eng.dma_start(out=t_wst[:, :], in_=w3tt[m].rearrange("p k m -> p (k m)")).then_inc(S["s_wl"], 16)
        nqh = HH * c.MI
        W(eng, "s_hd0", 16 * ((nqh + 1) // 2))
        W(eng, "s_hd1", 16 * (nqh // 2))
        W(eng, "s_x1d", 16 * c.FC * c.DD)
        for fc2 in range(c.FC2):
            W(eng, "s_ffn2", c.DD * fc2)
            for g in range(NGR):
                m0, m1_ = g * mper, min((g + 1) * mper, c.MI)
                eng.dma_start(out=t_hbig[:, m0:m1_, :],
                              in_=h_dram[:, m0:m1_, fc2 * c.NC2:(fc2 + 1) * c.NC2]).then_inc(
                                  S["s_hla" if g < (NGR + 1) // 2 else "s_hlb"], 16)
            W(eng, "s_z2", c.DD * fc2)
            eng.dma_start(out=t_x1c[:, :, :],
                          in_=x1_dram[:, :, fc2 * c.NC2:(fc2 + 1) * c.NC2]).then_inc(S["s_x1l"], 16)
            for do in range(c.DD):
                fl = fc2 * c.DD + do
                W(eng, "s_ffn2", fl - 1)
                eng.dma_start(out=t_w2bf[:, fl % 2].rearrange("p a b -> p (a b)"),
                              in_=w2bf_d[do]).then_inc(S["s_w2a" if fl % 2 == 0 else "s_w2b"], 16)

    @blk.gpsimd
    def _(eng):
        eng.dma_start(out=t_xpw[:, :, :], in_=xproj.rearrange("(k p) f -> p k f", p=P)).then_inc(S["s_gc"], 16)
        eng.dma_start(out=t_dtw[:, :], in_=dtw).then_inc(S["s_gc"], 16)
        eng.dma_start(out=t_araw[:, :, :], in_=a_sl).then_inc(S["s_gc"], 16)
        eng.dma_start(out=t_dtb[:, :], in_=dtb).then_inc(S["s_gc"], 16)
        eng.dma_start(out=t_dsl[:, :], in_=dsl).then_inc(S["s_gc"], 16)
        eng.dma_start(out=t_w1sc[:, :], in_=w1sp).then_inc(S["s_gc"], 16)
        eng.dma_start(out=t_w3sc[:, :], in_=w3sp).then_inc(S["s_gc"], 16)
        eng.dma_start(out=t_w2sc[:, :], in_=w2sp).then_inc(S["s_gc"], 16)
        for cc in range(NCH):
            W(eng, "s_xpe", cc + 1)
            eng.dma_start(out=xp_part[cc, 0:c.R, :], in_=t_dtins[:, :]).then_inc(S["s_gw"], 16)
            eng.dma_start(out=xp_part[cc, c.R:160, :], in_=t_bcs[:, :]).then_inc(S["s_gw"], 16)
            eng.wait_ge(S["s_gw"], 32 * (cc + 1))
            eng.collective_compute(
                "AllReduce", ALU.add, replica_groups=[list(range(NCORES))],
                ins=[xp_part[cc].opt()], outs=[xp_red[cc].opt()],
            ).then_inc(S["s_cc"], 1)
            eng.wait_ge(S["s_cc"], cc + 1)
            W(eng, "s_dt", KD * (cc - 1))
            eng.dma_start(out=t_dtin[:, cc % 2],
                          in_=xp_red[cc, 0:c.R, :]).then_inc(S["s_gr0" if cc % 2 == 0 else "s_gr1"], 16)
            if cc > 0:
                for b in range(slot_lo[(cc - 1) * DST], slot_lo[cc * DST]):
                    W(eng, "s_wcv", b + 1)
                    srcap, dstap, w_ = conv_blocks[b]
                    eng.dma_start(out=dstap, in_=t_cbf[:, b % 2][:, 0:w_]).then_inc(
                        S["s_wcs0" if b % 2 == 0 else "s_wcs1"], 16)
        for b in range(slot_lo[(NCH - 1) * DST], slot_lo[NCH * DST]):
            W(eng, "s_wcv", b + 1)
            srcap, dstap, w_ = conv_blocks[b]
            eng.dma_start(out=dstap, in_=t_cbf[:, b % 2][:, 0:w_]).then_inc(
                S["s_wcs0" if b % 2 == 0 else "s_wcs1"], 16)
        eng.wait_ge(S["s_wcs0"], 16 * ((NBLK + 1) // 2))
        if NBLK // 2 > 0:
            eng.wait_ge(S["s_wcs1"], 16 * (NBLK // 2))
        nq = KD * NCH
        eng.wait_ge(S["s_zd0"], 16 * ((nq + 1) // 2))
        if nq // 2 > 0:
            eng.wait_ge(S["s_zd1"], 16 * (nq // 2))
        eng.wait_ge(S["s_gr0"], 16 * ((NCH + 1) // 2))
        if NCH // 2 > 0:
            eng.wait_ge(S["s_gr1"], 16 * (NCH // 2))
        eng.collective_compute(
            "AllToAll", ALU.bypass, replica_groups=[list(range(NCORES))],
            ins=[a2a_in.opt()], outs=[a2a_out.opt()],
        ).then_inc(S["s_cc"], 1)

        for m in range(c.MI):
            for half in range(HH):
                q = HH * m + half
                W(eng, "s_h", q + 1)
                eng.dma_start(out=h_dram[:, m, half * c.HU:(half + 1) * c.HU],
                              in_=t_hbuf[:, q % 2]).then_inc(S["s_hd0" if q % 2 == 0 else "s_hd1"], 16)
        z2d2 = [z2_dram, z2_dram2]

        def p2load(fc2x, do):
            fl = fc2x * c.DD + do
            W(eng, "s_o", fl - 1)
            W(eng, "s_z2d0" if fl % 2 == 0 else "s_z2d1", 16 * (fl // 2 + 1))
            eng.dma_start(out=t_z2l[:, do % 2],
                          in_=z2d2[fc2x % 2][:, do, :]).then_inc(S["s_z2l0" if fl % 2 == 0 else "s_z2l1"], 16)
            W(eng, "s_o", fl + 1)
            eng.dma_start(out=out_e[do * P:(do + 1) * P, fc2x * c.NC2:(fc2x + 1) * c.NC2],
                          in_=t_outb[:, fl % 2]).then_inc(S["s_out0" if fl % 2 == 0 else "s_out1"], 16)

        for fc2 in range(c.FC2):
            for do in range(c.DD):
                fl = fc2 * c.DD + do
                W(eng, "s_z2", fl + 1)
                W(eng, "s_o", (fc2 - 1) * c.DD)
                eng.dma_start(out=z2d2[fc2 % 2][:, do, :],
                              in_=t_z2z[:, fl % 2, 0, :]).then_inc(S["s_z2d0" if fl % 2 == 0 else "s_z2d1"], 16)
                if fc2 >= 1:
                    p2load(fc2 - 1, do)
        for do in range(c.DD):
            p2load(c.FC2 - 1, do)
        nfl = c.FC2 * c.DD
        eng.wait_ge(S["s_out0"], 16 * ((nfl + 1) // 2))
        if nfl // 2 > 0:
            eng.wait_ge(S["s_out1"], 16 * (nfl // 2))

    @blk.tensor
    def _(eng):
        def xp_mm(cc):
            last = None
            for kk in range(KD):
                for j in range(NT):
                    sl = slice(j * NSUB, (j + 1) * NSUB)
                    eng.matmul(ps_xp1[:, sl], t_xpw[:, kk, 0:P],
                               t_xbuf[:, cc % 2, kk, sl],
                               start=(kk == 0), stop=(kk == KD - 1))
                    last = eng.matmul(ps_xp2[:, sl], t_xpw[:, kk, P:160],
                                      t_xbuf[:, cc % 2, kk, sl],
                                      start=(kk == 0), stop=(kk == KD - 1), skip_group_check=True)
            last.then_inc(S["s_xp"], 1)

        eng.wait_ge(S["s_gc"], 16 * NGC)
        eng.wait_ge(S[XS[0]], 16)
        xp_mm(0)
        for cc in range(NCH):
            W(eng, "s_gr0" if cc % 2 == 0 else "s_gr1", 16 * (cc // 2 + 1))
            for kk in range(KD):
                W(eng, "s_sp", cc * KD + kk - 1)
                last = None
                for j in range(NT):
                    sl = slice(j * NSUB, (j + 1) * NSUB)
                    last = eng.matmul(ps_dt[kk % 2][:, sl], t_dtw[:, kk * P:(kk + 1) * P],
                                      t_dtin[:, cc % 2, sl], start=True, stop=True,
                                      skip_group_check=True)
                last.then_inc(S["s_dt"], 1)
            if cc + 1 < NCH:
                i = cc + 1
                W(eng, XS[i % 2], 16 * (i // 2 + 1))
                W(eng, "s_xpe", cc + 1)
                xp_mm(cc + 1)

        W(eng, "s_vpr", 1)
        for fc in range(c.FC):
            for dd in range(c.DD):
                fl = fc * c.DD + dd
                if dd == 0:
                    W(eng, "s_zl0" if fc % 2 == 0 else "s_zl1", 16 * 2 * (fc // 2 + 1))
                    W(eng, "s_str", fc)
                W(eng, "s_sq1", fl + 1)
                eng.matmul(ps_stat_z[:, :], t_ones_col[:, :], t_zfull[:, fc % 2, dd],
                           start=(dd == 0), stop=(dd == c.DD - 1), skip_group_check=True)
                eng.matmul(ps_stat_q[:, :], t_ones_col[:, :], t_zsqp[:, fl % 4],
                           start=(dd == 0), stop=(dd == c.DD - 1),
                           skip_group_check=True).then_inc(S["s_stat"], 1)
            W(eng, "s_st", fc + 1)
            W(eng, "s_x1", fc * c.DD)
            eng.matmul(ps_m1[:, :], t_ones_row[:, :], t_mrow[:, 0:c.ZC], start=True, stop=True,
                       skip_group_check=True)
            eng.matmul(ps_r1[:, :], t_ones_row[:, :], t_rrow[:, 0:c.ZC], start=True, stop=True,
                       skip_group_check=True).then_inc(S["s_rep1"], 1)

        W(eng, "s_x1", c.FC * c.DD)
        nun = c.HU // NSUB if c.HU >= NSUB else 1
        nsub1 = min(NSUB, c.HU)
        for m in range(c.MI):
            W(eng, "s_cv", 2 * (m + 1))
            for half in range(HH):
                q = HH * m + half
                par = q % 2
                W(eng, "s_sil", q - 1)
                W(eng, "s_h", q - 1)
                last = None
                for j in range(nun):
                    toff = half * c.HU + j * nsub1
                    sl = slice(j * nsub1, (j + 1) * nsub1)
                    for kk in range(c.DD):
                        eng.matmul(ps_up[par][:, sl], t_w1bf[:, m % 2, kk],
                                   t_x1[:, kk, toff:toff + nsub1],
                                   start=(kk == 0), stop=(kk == c.DD - 1), skip_group_check=True)
                    for kk in range(c.DD):
                        last = eng.matmul(ps_gt[par][:, sl], t_w3bf[:, m % 2, kk],
                                          t_x1[:, kk, toff:toff + nsub1],
                                          start=(kk == 0), stop=(kk == c.DD - 1),
                                          skip_group_check=True)
                last.then_inc(S["s_ffn1"], 1)

        W(eng, "s_h", HH * c.MI)
        W(eng, "s_sil", HH * c.MI)
        for fc2 in range(c.FC2):
            def stats2_mm(fl):
                do_ = fl % c.DD
                W(eng, "s_sq2", fl + 1)
                if do_ == 0:
                    W(eng, "s_str2", fc2)
                eng.matmul(ps_st2_z[:, :], t_ones_col[:, :], t_z2z[:, fl % 2, 0, :],
                           start=(do_ == 0), stop=(do_ == c.DD - 1), skip_group_check=True)
                eng.matmul(ps_st2_q[:, :], t_ones_col[:, :], t_z2z[:, fl % 2, 1, :],
                           start=(do_ == 0), stop=(do_ == c.DD - 1),
                           skip_group_check=True).then_inc(S["s_stat2"], 1)

            for do in range(c.DD):
                fl = fc2 * c.DD + do
                W(eng, "s_w2a" if fl % 2 == 0 else "s_w2b", 16 * (fl // 2 + 1))
                W(eng, "s_z2", fl - 1)
                nga = (NGR + 1) // 2
                last = None
                for mk in range(c.MI):
                    if do == 0 and mk == 0:
                        W(eng, "s_hla", 16 * nga * (fc2 + 1))
                    if do == 0 and mk == nga * mper and NGR > nga:
                        W(eng, "s_hlb", 16 * (NGR - nga) * (fc2 + 1))
                    last = eng.matmul(ps_o2[fl % 2][:, :], t_w2bf[:, fl % 2, mk], t_hbig[:, mk, :],
                                      start=(mk == 0), stop=(mk == c.MI - 1), skip_group_check=True)
                last.then_inc(S["s_ffn2"], 1)
                if do >= 2:
                    stats2_mm(fl - 2)
            stats2_mm(fc2 * c.DD + c.DD - 2)
            stats2_mm(fc2 * c.DD + c.DD - 1)
            W(eng, "s_st2", fc2 + 1)
            W(eng, "s_o", fc2 * c.DD)
            eng.matmul(ps_m2[:, :], t_ones_row[:, :], t_mrow[:, 0:c.NC2], start=True, stop=True,
                       skip_group_check=True)
            eng.matmul(ps_r2[:, :], t_ones_row[:, :], t_rrow[:, 0:c.NC2], start=True, stop=True,
                       skip_group_check=True).then_inc(S["s_rep2"], 1)

    @blk.scalar
    def _(eng):
        eng = _Chain(eng, S["s_sch"])
        eng.wait_ge(S["s_gc"], 16 * NGC)
        eng.activation(t_ta[:, :, :], t_araw[:, :, :], AF.Exp).then_inc(S["s_prc"], 1)
        W(eng, "s_vpr", 1)

        def softplus(cc):
            for kk in range(KD):
                W(eng, "s_dt", cc * KD + kk + 1)
                W(eng, "s_dts", (cc - 1) * KD + kk + 1)
                b = t_dtb[:, kk:kk + 1]
                eng.activation(t_t02[:, :], ps_dt[kk % 2], AF.Abs, bias=b)
                eng.activation(t_t1[:, kk], ps_dt[kk % 2], AF.Relu, bias=b)
                eng.activation(t_t02[:, :], t_t02[:, :], AF.Exp, scale=t_cns[:, 0:1])
                eng.activation(t_t3[:, kk], t_t02[:, :], AF.Ln, bias=t_cns[:, 1:2]).then_inc(S["s_sp"], 1)

        softplus(0)
        for cc in range(NCH):
            for n in range(DST):
                for kk in range(KD):
                    flat = (cc * DST + n) * KD + kk
                    W(eng, "s_hts", flat - 1)
                    W(eng, "s_dts", cc * KD + kk + 1)
                    eng.activation(t_dA[:, flat % 2], t_dt[:, kk], AF.Exp,
                                   scale=t_aneg[:, kk, n:n + 1]).then_inc(S["s_da"], 1)
                flat_s = cc * DST + n
                for b in range(slot_lo[flat_s], slot_lo[flat_s + 1]):
                    W(eng, "s_wcl0" if b % 2 == 0 else "s_wcl1", 16 * (b // 2 + 1))
                    W(eng, "s_wcs0" if b % 2 == 0 else "s_wcs1", 16 * (b // 2))
                    w_ = conv_blocks[b][2]
                    eng.copy(t_cbf[:, b % 2][:, 0:w_], t_cst[:, b % 2][:, 0:w_]).then_inc(S["s_wcv"], 1)
                if n == 7 and cc + 1 < NCH:
                    softplus(cc + 1)

        for fc in range(c.FC):
            for dd in range(c.DD):
                fl = fc * c.DD + dd
                if dd == 0:
                    W(eng, "s_zl0" if fc % 2 == 0 else "s_zl1", 16 * 2 * (fc // 2 + 1))
                W(eng, "s_stat", fl - 3)
                eng.activation(t_zsqp[:, fl % 4], t_zfull[:, fc % 2, dd], AF.Square).then_inc(S["s_sq1"], 1)
            W(eng, "s_str", fc + 1)
            eng.activation(t_rrow[:, 0:c.ZC], t_vrow[:, 0:c.ZC], AF.Ln, bias=t_cns[0:1, 2:3])
            eng.activation(t_rrow[:, 0:c.ZC], t_rrow[:, 0:c.ZC], AF.Exp, scale=t_cns[0:1, 3:4]).then_inc(S["s_st"], 1)
        for m in range(-2, c.MI):
            mc = m + 2
            if mc < c.MI:
                W(eng, "s_wl", 16 * (2 * mc + 1))
                W(eng, "s_ffn1", HH * (mc - 1))
                eng.copy(t_w1bf[:, mc % 2].rearrange("p a b -> p (a b)"), t_wst[:, :]).then_inc(S["s_cv"], 1)
                W(eng, "s_wl", 16 * (2 * mc + 2))
                eng.copy(t_w3bf[:, mc % 2].rearrange("p a b -> p (a b)"), t_wst[:, :]).then_inc(S["s_cv"], 1)
            if m < 0:
                continue
            for half in range(HH):
                q = HH * m + half
                W(eng, "s_ffn1", q + 1)
                W(eng, "s_h", q - 1)
                eng.activation(t_silu[:, q % 2], ps_up[q % 2], AF.Silu,
                               scale=t_w1sc[:, m:m + 1]).then_inc(S["s_sil"], 1)
        for fc2 in range(c.FC2):
            def z2sq(fl):
                W(eng, "s_z2", fl + 1)
                W(eng, "s_stat2", fl - 1)
                eng.activation(t_z2z[:, fl % 2, 1, :], t_z2z[:, fl % 2, 0, :], AF.Square).then_inc(S["s_sq2"], 1)

            for do in range(c.DD):
                fl = fc2 * c.DD + do
                if do >= 2:
                    z2sq(fl - 2)
            z2sq(fc2 * c.DD + c.DD - 2)
            z2sq(fc2 * c.DD + c.DD - 1)
            W(eng, "s_str2", fc2 + 1)
            eng.activation(t_rrow[:, 0:c.NC2], t_vrow[:, 0:c.NC2], AF.Ln, bias=t_cns[0:1, 2:3])
            eng.activation(t_rrow[:, 0:c.NC2], t_rrow[:, 0:c.NC2], AF.Exp,
                           scale=t_cns[0:1, 3:4]).then_inc(S["s_st2"], 1)

    @blk.vector
    def _(eng):
        eng = _Chain(eng, S["s_vch"])
        eng.memset(t_ones_col[:, :], 1.0)
        eng.memset(t_ones_row[:, :], 1.0)
        eng.memset(t_cns[:, 0:1], -1.0)
        eng.memset(t_cns[:, 1:2], 1.0)
        eng.memset(t_cns[:, 2:3], 1e-5)
        eng.memset(t_cns[:, 3:4], -0.5).then_inc(S["s_vpr"], 1)
        eng.wait_ge(S["s_gc"], 16 * NGC)
        W(eng, "s_prc", 1)
        eng.tensor_scalar_mul(t_aneg[:, :, :], t_ta[:, :, :], -1.0)
        eng.tensor_scalar_add(t_d1p[:, :], t_dsl[:, :], 1.0)

        eng.wait_ge(S["s_xp"], 1)
        eng.tensor_copy(t_dtins[:, :], ps_xp1)
        eng.tensor_copy(t_bcs[:, :], ps_xp2).then_inc(S["s_xpe"], 1)

        def finalize(cc2, kk):
            W(eng, "s_sp", cc2 * KD + kk + 1)
            eng.tensor_add(t_dt[:, kk], t_t1[:, kk], t_t3[:, kk])
            eng.tensor_mul(t_u[:, kk], t_dt[:, kk], t_xbuf[:, cc2 % 2, kk]).then_inc(S["s_dts"], 1)

        for kk in range(KD):
            finalize(0, kk)
        for cc in range(NCH):
            for n in range(DST):
                if n == 1 and cc + 1 < NCH:
                    W(eng, "s_xp", cc + 2)
                    W(eng, "s_gw", 32 * (cc + 1))
                    eng.tensor_copy(t_dtins[:, :], ps_xp1)
                    eng.tensor_copy(t_bcs[:, :], ps_xp2).then_inc(S["s_xpe"], 1)
                for kk in range(KD):
                    flat = (cc * DST + n) * KD + kk
                    W(eng, "s_da", flat + 1)
                    nn = cc * DST + n
                    W(eng, "s_bc0" if n % 2 == 0 else "s_bc1", 32 * (nn // 2 + 1))
                    eng.tensor_mul(t_dbx[:, :], t_u[:, kk], t_brep[:, n % 2])
                    init = 0.0 if (cc % c.NPB == 0) else t_state[:, kk, n:n + 1]
                    eng.tensor_tensor_scan(t_h[:, :], t_dA[:, flat % 2], t_dbx[:, :],
                                           init, op0=ALU.mult, op1=ALU.add).then_inc(S["s_hts"], 1)
                    eng.tensor_copy(t_state[:, kk, n:n + 1], t_h[:, Tc - 1:Tc])
                    if n == 0:
                        ins_ = eng.tensor_mul(t_y[:, kk], t_h[:, :], t_crep[:, n % 2])
                    else:
                        eng.tensor_mul(t_tmp[:, :], t_h[:, :], t_crep[:, n % 2])
                        ins_ = eng.tensor_add(t_y[:, kk], t_y[:, kk], t_tmp[:, :])
                    if kk == KD - 1:
                        ins_.then_inc(S["s_vbc"], 1)
                    if n == DST - 1 and cc + 1 < NCH:
                        finalize(cc + 1, kk)
            for kk in range(KD):
                q = cc * KD + kk
                W(eng, "s_zd0" if q % 2 == 0 else "s_zd1", 16 * (q // 2))
                eng.scalar_tensor_tensor(t_z[:, q % 2], t_xbuf[:, cc % 2, kk], t_d1p[:, kk:kk + 1],
                                         t_y[:, kk], op0=ALU.mult, op1=ALU.add).then_inc(S["s_z"], 1)

        for fc in range(c.FC):
            W(eng, "s_stat", (fc + 1) * c.DD)
            eng.tensor_scalar_mul(t_mrow[:, 0:c.ZC], ps_stat_z[:, :], 1.0 / c.D)
            eng.tensor_scalar_mul(t_vrow[:, 0:c.ZC], ps_stat_q[:, :], 1.0 / c.D)
            eng.tensor_mul(t_rrow[:, 0:c.ZC], t_mrow[:, 0:c.ZC], t_mrow[:, 0:c.ZC])
            eng.tensor_sub(t_vrow[:, 0:c.ZC], t_vrow[:, 0:c.ZC], t_rrow[:, 0:c.ZC]).then_inc(S["s_str"], 1)
            W(eng, "s_rep1", fc + 1)
            for dd in range(c.DD):
                eng.tensor_sub(t_tmpz[:, :], t_zfull[:, fc % 2, dd], ps_m1)
                eng.tensor_mul(t_x1[:, dd, fc * c.ZC:(fc + 1) * c.ZC], t_tmpz[:, :], ps_r1).then_inc(S["s_x1"], 1)
        for m in range(c.MI):
            for half in range(HH):
                q = HH * m + half
                W(eng, "s_sil", q + 1)
                W(eng, "s_hd0" if q % 2 == 0 else "s_hd1", 16 * (q // 2))
                eng.scalar_tensor_tensor(t_hbuf[:, q % 2], ps_gt[q % 2], t_w3sc[:, m:m + 1],
                                         t_silu[:, q % 2], op0=ALU.mult, op1=ALU.mult).then_inc(S["s_h"], 1)
        def pass2(fc2x, do):
            fl = fc2x * c.DD + do
            if do == 0:
                W(eng, "s_rep2", fc2x + 1)
            W(eng, "s_z2l0" if fl % 2 == 0 else "s_z2l1", 16 * (fl // 2 + 1))
            W(eng, "s_out0" if fl % 2 == 0 else "s_out1", 16 * (fl // 2))
            eng.tensor_sub(t_tmp2[:, :], t_z2l[:, do % 2], ps_m2)
            eng.tensor_mul(t_outb[:, fl % 2], t_tmp2[:, :], ps_r2).then_inc(S["s_o"], 1)

        for fc2 in range(c.FC2):
            for do in range(c.DD):
                fl = fc2 * c.DD + do
                W(eng, "s_ffn2", fl + 1)
                W(eng, "s_x1l", 16 * (fc2 + 1))
                W(eng, "s_stat2", fl - 1)
                W(eng, "s_z2d0" if fl % 2 == 0 else "s_z2d1", 16 * (fl // 2))
                eng.scalar_tensor_tensor(t_z2z[:, fl % 2, 0, :], ps_o2[fl % 2], t_w2sc[:, do:do + 1],
                                         t_x1c[:, do, :], op0=ALU.mult, op1=ALU.add).then_inc(S["s_z2"], 1)
                if fc2 >= 1:
                    pass2(fc2 - 1, do)
            W(eng, "s_stat2", (fc2 + 1) * c.DD)
            eng.tensor_scalar_mul(t_mrow[:, 0:c.NC2], ps_st2_z[:, :], 1.0 / c.D)
            eng.tensor_scalar_mul(t_vrow[:, 0:c.NC2], ps_st2_q[:, :], 1.0 / c.D)
            eng.tensor_mul(t_rrow[:, 0:c.NC2], t_mrow[:, 0:c.NC2], t_mrow[:, 0:c.NC2])
            eng.tensor_sub(t_vrow[:, 0:c.NC2], t_vrow[:, 0:c.NC2], t_rrow[:, 0:c.NC2]).then_inc(S["s_str2"], 1)
        for do in range(c.DD):
            pass2(c.FC2 - 1, do)

    return nc


def host_layout(cfg, L0_attn_x, L0_attn_A_log, L0_attn_D, L0_attn_x_proj_w,
                L0_attn_dt_proj_w, L0_attn_dt_proj_b,
                L0_ffn_w1_q, L0_ffn_w1_s, L0_ffn_w3_q, L0_ffn_w3_s,
                L0_ffn_w2_q, L0_ffn_w2_s):
    c = cfg
    f32 = np.float32
    x = np.asarray(L0_attn_x, f32).reshape(c.TALL, c.D)
    xt = np.ascontiguousarray(x.T)
    xpt = np.ascontiguousarray(np.asarray(L0_attn_x_proj_w, f32).T)
    dtwt = np.ascontiguousarray(np.asarray(L0_attn_dt_proj_w, f32).T)
    Al = np.asarray(L0_attn_A_log, f32)
    Dv = np.asarray(L0_attn_D, f32)
    btv = np.asarray(L0_attn_dt_proj_b, f32)
    w1 = np.asarray(L0_ffn_w1_q, f32)
    w3 = np.asarray(L0_ffn_w3_q, f32)
    w2 = np.asarray(L0_ffn_w2_q, f32)
    w1tt = np.ascontiguousarray(w1.reshape(c.MI, P, c.DD, P).transpose(0, 3, 2, 1))
    w3tt = np.ascontiguousarray(w3.reshape(c.MI, P, c.DD, P).transpose(0, 3, 2, 1))
    w2tt = np.ascontiguousarray(w2.reshape(c.DD, P, c.MI, P).transpose(0, 3, 2, 1)).reshape(c.DD, P, c.MI * P)
    w1s = np.ascontiguousarray(np.asarray(L0_ffn_w1_s, f32).reshape(c.MI, P).T)
    w3s = np.ascontiguousarray(np.asarray(L0_ffn_w3_s, f32).reshape(c.MI, P).T)
    w2s = np.ascontiguousarray(np.asarray(L0_ffn_w2_s, f32).reshape(c.DD, P).T)

    in_maps = []
    for i in range(NCORES):
        sl = slice(i * c.DSL, (i + 1) * c.DSL)
        in_maps.append({
            "x_sl": np.ascontiguousarray(xt[sl]),
            "xproj": np.ascontiguousarray(xpt[sl]),
            "dtw": np.ascontiguousarray(dtwt[:, sl]),
            "a_sl": np.ascontiguousarray(Al[sl].reshape(c.KD, P, c.DST).transpose(1, 0, 2)),
            "dtb": np.ascontiguousarray(btv[sl].reshape(c.KD, P).T),
            "dsl": np.ascontiguousarray(Dv[sl].reshape(c.KD, P).T),
            "w1tt": w1tt, "w3tt": w3tt, "w2tt": w2tt,
            "w1sp": w1s, "w3sp": w3s, "w2sp": w2s,
        })
    return in_maps


def assemble(cfg, results):
    c = cfg
    out = np.empty((c.TALL, c.D), np.float32)
    for i in range(NCORES):
        out[i * c.TOK:(i + 1) * c.TOK, :] = np.asarray(results[i]["out"]).T
    return out.reshape(c.B, c.S, c.D)


def kernel(**inputs):
    cfg = full_cfg()
    in_maps = host_layout(cfg, **inputs)
    with allow_ici_all_to_all():
        nc = build_nc(cfg)
        res = run_bass_kernel_spmd(nc, in_maps, core_ids=list(range(NCORES)))
    return assemble(cfg, res.results)


# version 05
# speedup vs baseline: 1.0825x; 1.0825x over previous
import contextlib
import numpy as np
import concourse.bass as bass
import concourse.mybir as mybir
from concourse.bass_utils import run_bass_kernel_spmd
from concourse.aot_env import allow_ici_all_to_all

F32 = mybir.dt.float32
BF16 = mybir.dt.bfloat16
AF = mybir.ActivationFunctionType
ALU = mybir.AluOpType

NCORES = 8
P = 128


class Cfg:
    def __init__(self, D, B, S, I, Tc, ZC, NC2, HU):
        self.D, self.B, self.S, self.I = D, B, S, I
        self.R, self.DST = 128, 16
        self.Tc, self.ZC, self.NC2, self.HU = Tc, ZC, NC2, HU
        self.DSL = D // NCORES
        self.KD = self.DSL // P
        self.TALL = B * S
        self.TOK = self.TALL // NCORES
        self.NCH = self.TALL // Tc
        self.NPB = S // Tc
        self.CPD = self.TOK // Tc
        self.MI = I // P
        self.DD = D // P
        self.FC = self.TOK // ZC
        self.FC2 = self.TOK // NC2
        self.HH = self.TOK // HU
        self.NMM = 512
        assert self.DSL % P == 0 and S % Tc == 0 and self.TOK % Tc == 0
        assert self.TOK % ZC == 0 and self.TOK % NC2 == 0 and self.TOK % HU == 0
        assert I % P == 0 and D % P == 0 and ZC <= 512 and NC2 <= 512 and HU <= 1024


def full_cfg():
    return Cfg(D=4096, B=2, S=8192, I=11008, Tc=1024, ZC=128, NC2=512, HU=1024)


def _cdiv(a, b):
    return (a + b - 1) // b


def build_nc(cfg):
    c = cfg
    nc = bass.Bass("TRN2", target_bir_lowering=False, debug=False)

    x_sl = nc.dram_tensor("x_sl", [c.DSL, c.TALL], F32, kind="ExternalInput").ap()
    xproj = nc.dram_tensor("xproj", [c.DSL, 160], F32, kind="ExternalInput").ap()
    dtw = nc.dram_tensor("dtw", [c.R, c.DSL], F32, kind="ExternalInput").ap()
    a_sl = nc.dram_tensor("a_sl", [P, c.KD, c.DST], F32, kind="ExternalInput").ap()
    dtb = nc.dram_tensor("dtb", [P, c.KD], F32, kind="ExternalInput").ap()
    dsl = nc.dram_tensor("dsl", [P, c.KD], F32, kind="ExternalInput").ap()
    w1tt = nc.dram_tensor("w1tt", [c.MI, P, c.DD, P], F32, kind="ExternalInput").ap()
    w3tt = nc.dram_tensor("w3tt", [c.MI, P, c.DD, P], F32, kind="ExternalInput").ap()
    w2tt = nc.dram_tensor("w2tt", [c.DD, P, c.MI * P], F32, kind="ExternalInput").ap()
    w1sp = nc.dram_tensor("w1sp", [P, c.MI], F32, kind="ExternalInput").ap()
    w3sp = nc.dram_tensor("w3sp", [P, c.MI], F32, kind="ExternalInput").ap()
    w2sp = nc.dram_tensor("w2sp", [P, c.DD], F32, kind="ExternalInput").ap()
    out_e = nc.dram_tensor("out", [c.D, c.TOK], F32, kind="ExternalOutput").ap()

    xp_part = nc.dram_tensor("xp_part", [c.NCH, 160, c.Tc], F32).ap()
    xp_red = nc.dram_tensor("xp_red", [c.NCH, 160, c.Tc], F32).ap()
    a2a_in = nc.dram_tensor("a2a_in", [c.D, c.TOK], F32).ap()
    a2a_out = nc.dram_tensor("a2a_out", [c.D, c.TOK], F32).ap()
    h_dram = nc.dram_tensor("h_dram", [P, c.MI, c.TOK], BF16).ap()
    x1_dram = nc.dram_tensor("x1_dram", [P, c.DD, c.TOK], BF16).ap()
    z2_dram = nc.dram_tensor("z2_dram", [P, c.DD, c.NC2], F32).ap()
    w2bf_d = nc.dram_tensor("w2bf_d", [c.DD, P, c.MI * P], BF16).ap()
    z2_dram2 = nc.dram_tensor("z2_dram2", [P, c.DD, c.NC2], F32).ap()

    stack = contextlib.ExitStack()
    nc._cleanup_stack = stack

    arena = None
    CAP = None
    for ai, cand in enumerate(range(nc.SBUF_PARTITION_SIZE_BYTES, 150000, -2048)):
        try:
            arena = stack.enter_context(nc.sbuf_tensor(f"arena{ai}", [P, cand // 4], F32))
            CAP = cand
            break
        except (AssertionError, ValueError):
            continue
    assert arena is not None

    class Region:
        def __init__(self, base):
            self.off = base

        def alloc(self, nbytes):
            nbytes = (nbytes + 31) & ~31
            o = self.off
            self.off += nbytes
            assert self.off <= CAP, f"arena overflow {self.off} > {CAP}"
            return o

    def carve(off, shape, dtype=F32, p0=0, pn=P):
        esz = 4 if dtype == F32 else 2
        n = 1
        for s in shape:
            n *= s
        words = (n * esz + 3) // 4
        ap = arena[p0:p0 + pn, off // 4: off // 4 + words]
        if dtype != F32:
            ap = ap.bitcast(dtype)[:, :n]
        if len(shape) > 1:
            letters = "abcdefg"[: len(shape)]
            spec = "p (" + " ".join(letters) + ") -> p " + " ".join(letters)
            kw = {letters[i]: shape[i] for i in range(len(shape))}
            ap = ap.rearrange(spec, **kw)
        return ap

    RW = max(c.ZC, c.NC2)
    per = Region(0)
    t_state = carve(per.alloc(c.KD * c.DST * 4), [c.KD, c.DST])
    t_araw = carve(per.alloc(c.KD * c.DST * 4), [c.KD, c.DST])
    t_aneg = carve(per.alloc(c.KD * c.DST * 4), [c.KD, c.DST])
    t_ta = carve(per.alloc(c.KD * c.DST * 4), [c.KD, c.DST])
    t_dtb = carve(per.alloc(c.KD * 4), [c.KD])
    t_dsl = carve(per.alloc(c.KD * 4), [c.KD])
    t_d1p = carve(per.alloc(c.KD * 4), [c.KD])
    t_xpw = carve(per.alloc(c.KD * 160 * 4), [c.KD, 160])
    t_dtw = carve(per.alloc(c.DSL * 4), [c.DSL])
    t_w1sc = carve(per.alloc(c.MI * 4), [c.MI])
    t_w3sc = carve(per.alloc(c.MI * 4), [c.MI])
    t_w2sc = carve(per.alloc(c.DD * 4), [c.DD])
    t_ones_col = carve(per.alloc(4), [1])
    t_ones_row = carve(per.alloc(P * 4), [P], p0=0, pn=1)
    t_cns = carve(per.alloc(4 * 4), [4])
    t_mrow = carve(per.alloc(RW * 4), [RW], p0=0, pn=1)
    t_vrow = carve(per.alloc(RW * 4), [RW], p0=0, pn=1)
    t_rrow = carve(per.alloc(RW * 4), [RW], p0=0, pn=1)
    PER_END = per.off

    CB = 1024
    p1 = Region(PER_END)
    t_xbuf = carve(p1.alloc(2 * c.KD * c.Tc * 4), [2, c.KD, c.Tc])
    t_dtin = carve(p1.alloc(2 * c.Tc * 4), [2, c.Tc])
    t_dtins = carve(p1.alloc(c.Tc * 4), [c.Tc])
    t_bcs = carve(p1.alloc(c.Tc * 4), [c.Tc], p0=0, pn=32)
    t_dt = carve(p1.alloc(c.KD * c.Tc * 4), [c.KD, c.Tc])
    t_u = carve(p1.alloc(c.KD * c.Tc * 4), [c.KD, c.Tc])
    t_y = carve(p1.alloc(c.KD * c.Tc * 4), [c.KD, c.Tc])
    t_t1 = carve(p1.alloc(c.KD * c.Tc * 4), [c.KD, c.Tc])
    t_t3 = carve(p1.alloc(c.KD * c.Tc * 4), [c.KD, c.Tc])
    t_t02 = carve(p1.alloc(c.Tc * 4), [c.Tc])
    t_dA = carve(p1.alloc(2 * c.Tc * 4), [2, c.Tc])
    t_dbx = carve(p1.alloc(c.Tc * 4), [c.Tc])
    t_h = carve(p1.alloc(c.Tc * 4), [c.Tc])
    t_tmp = carve(p1.alloc(c.Tc * 4), [c.Tc])
    t_brep = carve(p1.alloc(2 * c.Tc * 4), [2, c.Tc])
    t_crep = carve(p1.alloc(2 * c.Tc * 4), [2, c.Tc])
    t_z = carve(p1.alloc(2 * c.Tc * 4), [2, c.Tc])
    t_cst = carve(p1.alloc(2 * CB * 4), [2, CB])
    t_cbf = carve(p1.alloc(2 * CB * 2), [2, CB], BF16)
    assert p1.off <= CAP

    f1 = Region(PER_END)
    t_x1 = carve(f1.alloc(c.DD * c.TOK * 2), [c.DD, c.TOK], BF16)
    X1_END = f1.off
    t_zfull = carve(f1.alloc(2 * c.DD * c.ZC * 4), [2, c.DD, c.ZC])
    t_zsqp = carve(f1.alloc(4 * c.ZC * 4), [4, c.ZC])
    t_tmpz = carve(f1.alloc(c.ZC * 4), [c.ZC])
    assert f1.off <= CAP
    f2a = Region(X1_END)
    t_wst = carve(f2a.alloc(c.DD * P * 4), [c.DD * P])
    t_w1bf = carve(f2a.alloc(2 * c.DD * P * 2), [2, c.DD, P], BF16)
    t_w3bf = carve(f2a.alloc(2 * c.DD * P * 2), [2, c.DD, P], BF16)
    t_silu = carve(f2a.alloc(2 * c.HU * 4), [2, c.HU])
    t_hbuf = carve(f2a.alloc(2 * c.HU * 2), [2, c.HU], BF16)
    assert f2a.off <= CAP
    f2b = Region(PER_END)
    t_hbig = carve(f2b.alloc(c.MI * c.NC2 * 2), [c.MI, c.NC2], BF16)
    t_w2bf = carve(f2b.alloc(2 * c.MI * P * 2), [2, c.MI, P], BF16)
    t_x1c = carve(f2b.alloc(c.DD * c.NC2 * 2), [c.DD, c.NC2], BF16)
    t_z2z = carve(f2b.alloc(2 * 2 * c.NC2 * 4), [2, 2, c.NC2])
    t_z2l = carve(f2b.alloc(2 * c.NC2 * 4), [2, c.NC2])
    t_outb = carve(f2b.alloc(2 * c.NC2 * 4), [2, c.NC2])
    t_tmp2 = carve(f2b.alloc(c.NC2 * 4), [c.NC2])
    assert f2b.off <= CAP

    ps = stack.enter_context(nc.psum_tensor("ps", [P, 4096], F32))
    BK = 512
    ps_xp1 = ps[:, 0:c.Tc]
    ps_xp2 = ps[0:32, 2 * BK:2 * BK + c.Tc]
    ps_dt = [ps[:, 4 * BK + 2 * i * BK: 4 * BK + 2 * i * BK + c.Tc] for i in range(2)]
    ps_stat_z = ps[0:1, 0:c.ZC]
    ps_stat_q = ps[0:1, BK:BK + c.ZC]
    ps_m1 = ps[:, 2 * BK:2 * BK + c.ZC]
    ps_r1 = ps[:, 3 * BK:3 * BK + c.ZC]
    ps_up = [ps[:, 0:c.HU], ps[:, 2 * BK:2 * BK + c.HU]]
    ps_gt = [ps[:, 4 * BK:4 * BK + c.HU], ps[:, 6 * BK:6 * BK + c.HU]]
    ps_o2 = [ps[:, 0:c.NC2], ps[:, BK:BK + c.NC2]]
    ps_st2_z = ps[0:1, 2 * BK:2 * BK + c.NC2]
    ps_st2_q = ps[0:1, 3 * BK:3 * BK + c.NC2]
    ps_m2 = ps[:, 4 * BK:4 * BK + c.NC2]
    ps_r2 = ps[:, 5 * BK:5 * BK + c.NC2]

    semnames = [
        "s_x0", "s_x1b", "s_bc0", "s_bc1", "s_zd0", "s_zd1",
        "s_xp", "s_dt", "s_xpe", "s_sp", "s_dts",
        "s_da", "s_hts", "s_vbc", "s_z", "s_gw", "s_gr0", "s_gr1", "s_cc", "s_gc",
        "s_prc", "s_vpr",
        "s_wcl0", "s_wcl1", "s_wcv", "s_wcs0", "s_wcs1",
        "s_zl0", "s_zl1", "s_stat", "s_str", "s_st", "s_rep1", "s_x1", "s_x1d",
        "s_wl", "s_cv", "s_ffn1", "s_sil", "s_h", "s_hd0", "s_hd1",
        "s_hla", "s_hlb", "s_x1l", "s_w2a", "s_w2b", "s_ffn2", "s_z2", "s_sq1", "s_sq2",
        "s_stat2", "s_str2", "s_st2", "s_rep2", "s_z2d0", "s_z2d1",
        "s_z2l0", "s_z2l1", "s_o", "s_out0", "s_out1", "s_vch", "s_sch",
    ]
    S = {}
    for nm in semnames:
        S[nm] = stack.enter_context(nc.semaphore(nm))
    XS = ["s_x0", "s_x1b"]

    DST, KD, Tc, NCH = c.DST, c.KD, c.Tc, c.NCH
    NSUB = min(c.NMM, Tc)
    NT = Tc // NSUB
    HH = c.HH
    NHG = min(4, c.MI)
    mper = _cdiv(c.MI, NHG)
    NGR = _cdiv(c.MI, mper)
    NGC = 8

    conv_blocks = []
    nb2 = _cdiv(c.MI * P, CB)
    for do in range(c.DD):
        for q in range(nb2):
            lo, hi = q * CB, min((q + 1) * CB, c.MI * P)
            conv_blocks.append((w2tt[do][:, lo:hi], w2bf_d[do][:, lo:hi], hi - lo))
    NBLK = len(conv_blocks)
    NSLOT = NCH * DST
    base, extra = divmod(NBLK, NSLOT)
    slot_lo = [0]
    for s_ in range(NSLOT):
        slot_lo.append(slot_lo[-1] + base + (1 if s_ < extra else 0))

    x_r = x_sl.rearrange("(k p) t -> p k t", p=P)

    def W(eng, sem, v):
        if v > 0:
            eng.wait_ge(S[sem], v)

    class _Proxy:
        def __init__(self, ch, ins):
            self._ch, self._ins = ch, ins

        def then_inc(self, sem, val=1):
            ch = self._ch
            assert ch._pending is self._ins
            n = ch._counts.get(sem.name, 0) + val
            ch._counts[sem.name] = n
            self._ins.then_inc(sem, val)
            ch._last = (sem, n)
            ch._pending = None
            return self

    class _Chain:

        def __init__(self, e, sem):
            self._e, self._sem = e, sem
            self._counts = {}
            self._pending = None
            self._last = None

        def _flush(self):
            if self._pending is not None:
                n = self._counts.get(self._sem.name, 0) + 1
                self._counts[self._sem.name] = n
                self._pending.then_inc(self._sem, 1)
                self._last = (self._sem, n)
                self._pending = None

        def wait_ge(self, sem, v):
            self._flush()
            self._e.wait_ge(sem, v)

        def __getattr__(self, name):
            f = getattr(self._e, name)
            ch = self

            def wrapped(*a, **k):
                ch._flush()
                if ch._last is not None:
                    ch._e.wait_ge(ch._last[0], ch._last[1])
                ins = f(*a, **k)
                ch._pending = ins
                return _Proxy(ch, ins)
            return wrapped

    blk = stack.enter_context(nc.Block())

    @blk.sync
    def _(eng):
        for i in range(min(2, NCH)):
            eng.dma_start(out=t_xbuf[:, i % 2], in_=x_r[:, :, i * Tc:(i + 1) * Tc]).then_inc(S[XS[i % 2]], 16)
        for cc in range(NCH):
            for n in range(DST):
                flat = cc * DST + n
                W(eng, "s_vbc", flat - 1)
                W(eng, "s_cc", cc + 1)
                src_b = xp_red[cc, c.R + n: c.R + n + 1, :].broadcast_to([P, Tc])
                src_c = xp_red[cc, c.R + DST + n: c.R + DST + n + 1, :].broadcast_to([P, Tc])
                eng.dma_start(out=t_brep[:, n % 2], in_=src_b).then_inc(S["s_bc0" if n % 2 == 0 else "s_bc1"], 16)
                eng.dma_start(out=t_crep[:, n % 2], in_=src_c).then_inc(S["s_bc0" if n % 2 == 0 else "s_bc1"], 16)
                for b in range(slot_lo[flat], slot_lo[flat + 1]):
                    W(eng, "s_wcv", b - 1)
                    srcap, dstap, w_ = conv_blocks[b]
                    eng.dma_start(out=t_cst[:, b % 2][:, 0:w_],
                                  in_=srcap).then_inc(S["s_wcl0" if b % 2 == 0 else "s_wcl1"], 16)
            j = cc // c.CPD
            coff = (cc % c.CPD) * Tc
            for kk in range(KD):
                q = cc * KD + kk
                W(eng, "s_z", q + 1)
                r0 = j * c.DSL + kk * P
                eng.dma_start(out=a2a_in[r0:r0 + P, coff:coff + Tc],
                              in_=t_z[:, q % 2]).then_inc(S["s_zd0" if q % 2 == 0 else "s_zd1"], 16)
            if cc + 2 < NCH:
                W(eng, "s_z", KD * (cc + 1))
                i = cc + 2
                eng.dma_start(out=t_xbuf[:, i % 2], in_=x_r[:, :, i * Tc:(i + 1) * Tc]).then_inc(S[XS[i % 2]], 16)

        a2o = a2a_out.rearrange("(dd p) t -> p dd t", p=P)
        W(eng, "s_cc", NCH + 1)
        for fc in range(c.FC):
            W(eng, "s_x1", (fc - 1) * c.DD)
            hd = c.DD // 2
            for g in range(2):
                eng.dma_start(out=t_zfull[:, fc % 2, g * hd:(g + 1) * hd],
                              in_=a2o[:, g * hd:(g + 1) * hd, fc * c.ZC:(fc + 1) * c.ZC]).then_inc(
                                  S["s_zl0" if fc % 2 == 0 else "s_zl1"], 16)
        for fc in range(c.FC):
            for dd in range(c.DD):
                W(eng, "s_x1", fc * c.DD + dd + 1)
                eng.dma_start(out=x1_dram[:, dd, fc * c.ZC:(fc + 1) * c.ZC],
                              in_=t_x1[:, dd, fc * c.ZC:(fc + 1) * c.ZC]).then_inc(S["s_x1d"], 16)
        W(eng, "s_x1", c.FC * c.DD)
        for m in range(c.MI):
            W(eng, "s_cv", 2 * m)
            eng.dma_start(out=t_wst[:, :], in_=w1tt[m].rearrange("p k m -> p (k m)")).then_inc(S["s_wl"], 16)
            W(eng, "s_cv", 2 * m + 1)
            eng.dma_start(out=t_wst[:, :], in_=w3tt[m].rearrange("p k m -> p (k m)")).then_inc(S["s_wl"], 16)
        nqh = HH * c.MI
        W(eng, "s_hd0", 16 * ((nqh + 1) // 2))
        W(eng, "s_hd1", 16 * (nqh // 2))
        W(eng, "s_x1d", 16 * c.FC * c.DD)
        for fc2 in range(c.FC2):
            W(eng, "s_ffn2", c.DD * fc2)
            for g in range(NGR):
                m0, m1_ = g * mper, min((g + 1) * mper, c.MI)
                eng.dma_start(out=t_hbig[:, m0:m1_, :],
                              in_=h_dram[:, m0:m1_, fc2 * c.NC2:(fc2 + 1) * c.NC2]).then_inc(
                                  S["s_hla" if g < (NGR + 1) // 2 else "s_hlb"], 16)
            W(eng, "s_z2", c.DD * fc2)
            eng.dma_start(out=t_x1c[:, :, :],
                          in_=x1_dram[:, :, fc2 * c.NC2:(fc2 + 1) * c.NC2]).then_inc(S["s_x1l"], 16)
            for do in range(c.DD):
                fl = fc2 * c.DD + do
                W(eng, "s_ffn2", fl - 1)
                eng.dma_start(out=t_w2bf[:, fl % 2].rearrange("p a b -> p (a b)"),
                              in_=w2bf_d[do]).then_inc(S["s_w2a" if fl % 2 == 0 else "s_w2b"], 16)

    @blk.gpsimd
    def _(eng):
        eng.dma_start(out=t_xpw[:, :, :], in_=xproj.rearrange("(k p) f -> p k f", p=P)).then_inc(S["s_gc"], 16)
        eng.dma_start(out=t_dtw[:, :], in_=dtw).then_inc(S["s_gc"], 16)
        eng.dma_start(out=t_araw[:, :, :], in_=a_sl).then_inc(S["s_gc"], 16)
        eng.dma_start(out=t_dtb[:, :], in_=dtb).then_inc(S["s_gc"], 16)
        eng.dma_start(out=t_dsl[:, :], in_=dsl).then_inc(S["s_gc"], 16)
        eng.dma_start(out=t_w1sc[:, :], in_=w1sp).then_inc(S["s_gc"], 16)
        eng.dma_start(out=t_w3sc[:, :], in_=w3sp).then_inc(S["s_gc"], 16)
        eng.dma_start(out=t_w2sc[:, :], in_=w2sp).then_inc(S["s_gc"], 16)
        for cc in range(NCH):
            W(eng, "s_xpe", cc + 1)
            eng.dma_start(out=xp_part[cc, 0:c.R, :], in_=t_dtins[:, :]).then_inc(S["s_gw"], 16)
            eng.dma_start(out=xp_part[cc, c.R:160, :], in_=t_bcs[:, :]).then_inc(S["s_gw"], 16)
            eng.wait_ge(S["s_gw"], 32 * (cc + 1))
            eng.collective_compute(
                "AllReduce", ALU.add, replica_groups=[list(range(NCORES))],
                ins=[xp_part[cc].opt()], outs=[xp_red[cc].opt()],
            ).then_inc(S["s_cc"], 1)
            eng.wait_ge(S["s_cc"], cc + 1)
            W(eng, "s_dt", KD * (cc - 1))
            eng.dma_start(out=t_dtin[:, cc % 2],
                          in_=xp_red[cc, 0:c.R, :]).then_inc(S["s_gr0" if cc % 2 == 0 else "s_gr1"], 16)
            if cc > 0:
                for b in range(slot_lo[(cc - 1) * DST], slot_lo[cc * DST]):
                    W(eng, "s_wcv", b + 1)
                    srcap, dstap, w_ = conv_blocks[b]
                    eng.dma_start(out=dstap, in_=t_cbf[:, b % 2][:, 0:w_]).then_inc(
                        S["s_wcs0" if b % 2 == 0 else "s_wcs1"], 16)
        for b in range(slot_lo[(NCH - 1) * DST], slot_lo[NCH * DST]):
            W(eng, "s_wcv", b + 1)
            srcap, dstap, w_ = conv_blocks[b]
            eng.dma_start(out=dstap, in_=t_cbf[:, b % 2][:, 0:w_]).then_inc(
                S["s_wcs0" if b % 2 == 0 else "s_wcs1"], 16)
        eng.wait_ge(S["s_wcs0"], 16 * ((NBLK + 1) // 2))
        if NBLK // 2 > 0:
            eng.wait_ge(S["s_wcs1"], 16 * (NBLK // 2))
        nq = KD * NCH
        eng.wait_ge(S["s_zd0"], 16 * ((nq + 1) // 2))
        if nq // 2 > 0:
            eng.wait_ge(S["s_zd1"], 16 * (nq // 2))
        eng.wait_ge(S["s_gr0"], 16 * ((NCH + 1) // 2))
        if NCH // 2 > 0:
            eng.wait_ge(S["s_gr1"], 16 * (NCH // 2))
        eng.collective_compute(
            "AllToAll", ALU.bypass, replica_groups=[list(range(NCORES))],
            ins=[a2a_in.opt()], outs=[a2a_out.opt()],
        ).then_inc(S["s_cc"], 1)

        def cvm(mc):
            W(eng, "s_wl", 16 * (2 * mc + 1))
            W(eng, "s_ffn1", HH * (mc - 1))
            eng.tensor_copy(t_w1bf[:, mc % 2].rearrange("p a b -> p (a b)"), t_wst[:, :]).then_inc(S["s_cv"], 1)
            W(eng, "s_wl", 16 * (2 * mc + 2))
            eng.tensor_copy(t_w3bf[:, mc % 2].rearrange("p a b -> p (a b)"), t_wst[:, :]).then_inc(S["s_cv"], 1)

        def hstores(m):
            for half in range(HH):
                q = HH * m + half
                W(eng, "s_h", q + 1)
                eng.dma_start(out=h_dram[:, m, half * c.HU:(half + 1) * c.HU],
                              in_=t_hbuf[:, q % 2]).then_inc(S["s_hd0" if q % 2 == 0 else "s_hd1"], 16)

        cvm(0)
        if c.MI > 1:
            cvm(1)
        for m in range(c.MI):
            if m >= 1:
                hstores(m - 1)
            if m + 2 < c.MI:
                cvm(m + 2)
        hstores(c.MI - 1)
        z2d2 = [z2_dram, z2_dram2]

        def p2load(fc2x, do):
            fl = fc2x * c.DD + do
            W(eng, "s_o", fl - 1)
            W(eng, "s_z2d0" if fl % 2 == 0 else "s_z2d1", 16 * (fl // 2 + 1))
            eng.dma_start(out=t_z2l[:, do % 2],
                          in_=z2d2[fc2x % 2][:, do, :]).then_inc(S["s_z2l0" if fl % 2 == 0 else "s_z2l1"], 16)
            W(eng, "s_o", fl + 1)
            eng.dma_start(out=out_e[do * P:(do + 1) * P, fc2x * c.NC2:(fc2x + 1) * c.NC2],
                          in_=t_outb[:, fl % 2]).then_inc(S["s_out0" if fl % 2 == 0 else "s_out1"], 16)

        for fc2 in range(c.FC2):
            for do in range(c.DD):
                fl = fc2 * c.DD + do
                W(eng, "s_z2", fl + 1)
                W(eng, "s_o", (fc2 - 1) * c.DD)
                eng.dma_start(out=z2d2[fc2 % 2][:, do, :],
                              in_=t_z2z[:, fl % 2, 0, :]).then_inc(S["s_z2d0" if fl % 2 == 0 else "s_z2d1"], 16)
                if fc2 >= 1:
                    p2load(fc2 - 1, do)
        for do in range(c.DD):
            p2load(c.FC2 - 1, do)
        nfl = c.FC2 * c.DD
        eng.wait_ge(S["s_out0"], 16 * ((nfl + 1) // 2))
        if nfl // 2 > 0:
            eng.wait_ge(S["s_out1"], 16 * (nfl // 2))

    @blk.tensor
    def _(eng):
        def xp_mm(cc):
            last = None
            for kk in range(KD):
                for j in range(NT):
                    sl = slice(j * NSUB, (j + 1) * NSUB)
                    eng.matmul(ps_xp1[:, sl], t_xpw[:, kk, 0:P],
                               t_xbuf[:, cc % 2, kk, sl],
                               start=(kk == 0), stop=(kk == KD - 1))
                    last = eng.matmul(ps_xp2[:, sl], t_xpw[:, kk, P:160],
                                      t_xbuf[:, cc % 2, kk, sl],
                                      start=(kk == 0), stop=(kk == KD - 1), skip_group_check=True)
            last.then_inc(S["s_xp"], 1)

        eng.wait_ge(S["s_gc"], 16 * NGC)
        eng.wait_ge(S[XS[0]], 16)
        xp_mm(0)
        for cc in range(NCH):
            W(eng, "s_gr0" if cc % 2 == 0 else "s_gr1", 16 * (cc // 2 + 1))
            for kk in range(KD):
                W(eng, "s_sp", cc * KD + kk - 1)
                last = None
                for j in range(NT):
                    sl = slice(j * NSUB, (j + 1) * NSUB)
                    last = eng.matmul(ps_dt[kk % 2][:, sl], t_dtw[:, kk * P:(kk + 1) * P],
                                      t_dtin[:, cc % 2, sl], start=True, stop=True,
                                      skip_group_check=True)
                last.then_inc(S["s_dt"], 1)
            if cc + 1 < NCH:
                i = cc + 1
                W(eng, XS[i % 2], 16 * (i // 2 + 1))
                W(eng, "s_xpe", cc + 1)
                xp_mm(cc + 1)

        W(eng, "s_vpr", 1)
        for fc in range(c.FC):
            for dd in range(c.DD):
                fl = fc * c.DD + dd
                if dd == 0:
                    W(eng, "s_zl0" if fc % 2 == 0 else "s_zl1", 16 * 2 * (fc // 2 + 1))
                    W(eng, "s_str", fc)
                W(eng, "s_sq1", fl + 1)
                eng.matmul(ps_stat_z[:, :], t_ones_col[:, :], t_zfull[:, fc % 2, dd],
                           start=(dd == 0), stop=(dd == c.DD - 1), skip_group_check=True)
                eng.matmul(ps_stat_q[:, :], t_ones_col[:, :], t_zsqp[:, fl % 4],
                           start=(dd == 0), stop=(dd == c.DD - 1),
                           skip_group_check=True).then_inc(S["s_stat"], 1)
            W(eng, "s_st", fc + 1)
            W(eng, "s_x1", fc * c.DD)
            eng.matmul(ps_m1[:, :], t_ones_row[:, :], t_mrow[:, 0:c.ZC], start=True, stop=True,
                       skip_group_check=True)
            eng.matmul(ps_r1[:, :], t_ones_row[:, :], t_rrow[:, 0:c.ZC], start=True, stop=True,
                       skip_group_check=True).then_inc(S["s_rep1"], 1)

        W(eng, "s_x1", c.FC * c.DD)
        nun = c.HU // NSUB if c.HU >= NSUB else 1
        nsub1 = min(NSUB, c.HU)
        for m in range(c.MI):
            W(eng, "s_cv", 2 * (m + 1))
            for half in range(HH):
                q = HH * m + half
                par = q % 2
                W(eng, "s_sil", q - 1)
                W(eng, "s_h", q - 1)
                last = None
                for j in range(nun):
                    toff = half * c.HU + j * nsub1
                    sl = slice(j * nsub1, (j + 1) * nsub1)
                    for kk in range(c.DD):
                        eng.matmul(ps_up[par][:, sl], t_w1bf[:, m % 2, kk],
                                   t_x1[:, kk, toff:toff + nsub1],
                                   start=(kk == 0), stop=(kk == c.DD - 1), skip_group_check=True)
                    for kk in range(c.DD):
                        last = eng.matmul(ps_gt[par][:, sl], t_w3bf[:, m % 2, kk],
                                          t_x1[:, kk, toff:toff + nsub1],
                                          start=(kk == 0), stop=(kk == c.DD - 1),
                                          skip_group_check=True)
                last.then_inc(S["s_ffn1"], 1)

        W(eng, "s_h", HH * c.MI)
        W(eng, "s_sil", HH * c.MI)
        for fc2 in range(c.FC2):
            def stats2_mm(fl):
                do_ = fl % c.DD
                W(eng, "s_sq2", fl + 1)
                if do_ == 0:
                    W(eng, "s_str2", fc2)
                eng.matmul(ps_st2_z[:, :], t_ones_col[:, :], t_z2z[:, fl % 2, 0, :],
                           start=(do_ == 0), stop=(do_ == c.DD - 1), skip_group_check=True)
                eng.matmul(ps_st2_q[:, :], t_ones_col[:, :], t_z2z[:, fl % 2, 1, :],
                           start=(do_ == 0), stop=(do_ == c.DD - 1),
                           skip_group_check=True).then_inc(S["s_stat2"], 1)

            for do in range(c.DD):
                fl = fc2 * c.DD + do
                W(eng, "s_w2a" if fl % 2 == 0 else "s_w2b", 16 * (fl // 2 + 1))
                W(eng, "s_z2", fl - 1)
                nga = (NGR + 1) // 2
                last = None
                for mk in range(c.MI):
                    if do == 0 and mk == 0:
                        W(eng, "s_hla", 16 * nga * (fc2 + 1))
                    if do == 0 and mk == nga * mper and NGR > nga:
                        W(eng, "s_hlb", 16 * (NGR - nga) * (fc2 + 1))
                    last = eng.matmul(ps_o2[fl % 2][:, :], t_w2bf[:, fl % 2, mk], t_hbig[:, mk, :],
                                      start=(mk == 0), stop=(mk == c.MI - 1), skip_group_check=True)
                last.then_inc(S["s_ffn2"], 1)
                if do >= 2:
                    stats2_mm(fl - 2)
            stats2_mm(fc2 * c.DD + c.DD - 2)
            stats2_mm(fc2 * c.DD + c.DD - 1)
            W(eng, "s_st2", fc2 + 1)
            W(eng, "s_o", fc2 * c.DD)
            eng.matmul(ps_m2[:, :], t_ones_row[:, :], t_mrow[:, 0:c.NC2], start=True, stop=True,
                       skip_group_check=True)
            eng.matmul(ps_r2[:, :], t_ones_row[:, :], t_rrow[:, 0:c.NC2], start=True, stop=True,
                       skip_group_check=True).then_inc(S["s_rep2"], 1)

    @blk.scalar
    def _(eng):
        eng = _Chain(eng, S["s_sch"])
        eng.wait_ge(S["s_gc"], 16 * NGC)
        eng.activation(t_ta[:, :, :], t_araw[:, :, :], AF.Exp).then_inc(S["s_prc"], 1)
        W(eng, "s_vpr", 1)

        def softplus(cc):
            for kk in range(KD):
                W(eng, "s_dt", cc * KD + kk + 1)
                W(eng, "s_dts", (cc - 1) * KD + kk + 1)
                b = t_dtb[:, kk:kk + 1]
                eng.activation(t_t02[:, :], ps_dt[kk % 2], AF.Abs, bias=b)
                eng.activation(t_t1[:, kk], ps_dt[kk % 2], AF.Relu, bias=b)
                eng.activation(t_t02[:, :], t_t02[:, :], AF.Exp, scale=t_cns[:, 0:1])
                eng.activation(t_t3[:, kk], t_t02[:, :], AF.Ln, bias=t_cns[:, 1:2]).then_inc(S["s_sp"], 1)

        softplus(0)
        for cc in range(NCH):
            for n in range(DST):
                for kk in range(KD):
                    flat = (cc * DST + n) * KD + kk
                    W(eng, "s_hts", flat - 1)
                    W(eng, "s_dts", cc * KD + kk + 1)
                    eng.activation(t_dA[:, flat % 2], t_dt[:, kk], AF.Exp,
                                   scale=t_aneg[:, kk, n:n + 1]).then_inc(S["s_da"], 1)
                flat_s = cc * DST + n
                for b in range(slot_lo[flat_s], slot_lo[flat_s + 1]):
                    W(eng, "s_wcl0" if b % 2 == 0 else "s_wcl1", 16 * (b // 2 + 1))
                    W(eng, "s_wcs0" if b % 2 == 0 else "s_wcs1", 16 * (b // 2))
                    w_ = conv_blocks[b][2]
                    eng.copy(t_cbf[:, b % 2][:, 0:w_], t_cst[:, b % 2][:, 0:w_]).then_inc(S["s_wcv"], 1)
                if n == 7 and cc + 1 < NCH:
                    softplus(cc + 1)

        for fc in range(c.FC):
            for dd in range(c.DD):
                fl = fc * c.DD + dd
                if dd == 0:
                    W(eng, "s_zl0" if fc % 2 == 0 else "s_zl1", 16 * 2 * (fc // 2 + 1))
                W(eng, "s_stat", fl - 3)
                eng.activation(t_zsqp[:, fl % 4], t_zfull[:, fc % 2, dd], AF.Square).then_inc(S["s_sq1"], 1)
            W(eng, "s_str", fc + 1)
            eng.activation(t_rrow[:, 0:c.ZC], t_vrow[:, 0:c.ZC], AF.Ln, bias=t_cns[0:1, 2:3])
            eng.activation(t_rrow[:, 0:c.ZC], t_rrow[:, 0:c.ZC], AF.Exp, scale=t_cns[0:1, 3:4]).then_inc(S["s_st"], 1)
        for m in range(c.MI):
            for half in range(HH):
                q = HH * m + half
                W(eng, "s_ffn1", q + 1)
                W(eng, "s_h", q - 1)
                eng.activation(t_silu[:, q % 2], ps_up[q % 2], AF.Silu,
                               scale=t_w1sc[:, m:m + 1]).then_inc(S["s_sil"], 1)
        for fc2 in range(c.FC2):
            def z2sq(fl):
                W(eng, "s_z2", fl + 1)
                W(eng, "s_stat2", fl - 1)
                eng.activation(t_z2z[:, fl % 2, 1, :], t_z2z[:, fl % 2, 0, :], AF.Square).then_inc(S["s_sq2"], 1)

            for do in range(c.DD):
                fl = fc2 * c.DD + do
                if do >= 2:
                    z2sq(fl - 2)
            z2sq(fc2 * c.DD + c.DD - 2)
            z2sq(fc2 * c.DD + c.DD - 1)
            W(eng, "s_str2", fc2 + 1)
            eng.activation(t_rrow[:, 0:c.NC2], t_vrow[:, 0:c.NC2], AF.Ln, bias=t_cns[0:1, 2:3])
            eng.activation(t_rrow[:, 0:c.NC2], t_rrow[:, 0:c.NC2], AF.Exp,
                           scale=t_cns[0:1, 3:4]).then_inc(S["s_st2"], 1)

    @blk.vector
    def _(eng):
        eng = _Chain(eng, S["s_vch"])
        eng.memset(t_ones_col[:, :], 1.0)
        eng.memset(t_ones_row[:, :], 1.0)
        eng.memset(t_cns[:, 0:1], -1.0)
        eng.memset(t_cns[:, 1:2], 1.0)
        eng.memset(t_cns[:, 2:3], 1e-5)
        eng.memset(t_cns[:, 3:4], -0.5).then_inc(S["s_vpr"], 1)
        eng.wait_ge(S["s_gc"], 16 * NGC)
        W(eng, "s_prc", 1)
        eng.tensor_scalar_mul(t_aneg[:, :, :], t_ta[:, :, :], -1.0)
        eng.tensor_scalar_add(t_d1p[:, :], t_dsl[:, :], 1.0)

        eng.wait_ge(S["s_xp"], 1)
        eng.tensor_copy(t_dtins[:, :], ps_xp1)
        eng.tensor_copy(t_bcs[:, :], ps_xp2).then_inc(S["s_xpe"], 1)

        def finalize(cc2, kk):
            W(eng, "s_sp", cc2 * KD + kk + 1)
            eng.tensor_add(t_dt[:, kk], t_t1[:, kk], t_t3[:, kk])
            eng.tensor_mul(t_u[:, kk], t_dt[:, kk], t_xbuf[:, cc2 % 2, kk]).then_inc(S["s_dts"], 1)

        for kk in range(KD):
            finalize(0, kk)
        for cc in range(NCH):
            for n in range(DST):
                if n == 1 and cc + 1 < NCH:
                    W(eng, "s_xp", cc + 2)
                    W(eng, "s_gw", 32 * (cc + 1))
                    eng.tensor_copy(t_dtins[:, :], ps_xp1)
                    eng.tensor_copy(t_bcs[:, :], ps_xp2).then_inc(S["s_xpe"], 1)
                for kk in range(KD):
                    flat = (cc * DST + n) * KD + kk
                    W(eng, "s_da", flat + 1)
                    nn = cc * DST + n
                    W(eng, "s_bc0" if n % 2 == 0 else "s_bc1", 32 * (nn // 2 + 1))
                    eng.tensor_mul(t_dbx[:, :], t_u[:, kk], t_brep[:, n % 2])
                    init = 0.0 if (cc % c.NPB == 0) else t_state[:, kk, n:n + 1]
                    eng.tensor_tensor_scan(t_h[:, :], t_dA[:, flat % 2], t_dbx[:, :],
                                           init, op0=ALU.mult, op1=ALU.add).then_inc(S["s_hts"], 1)
                    eng.tensor_copy(t_state[:, kk, n:n + 1], t_h[:, Tc - 1:Tc])
                    if n == 0:
                        ins_ = eng.tensor_mul(t_y[:, kk], t_h[:, :], t_crep[:, n % 2])
                    else:
                        eng.tensor_mul(t_tmp[:, :], t_h[:, :], t_crep[:, n % 2])
                        ins_ = eng.tensor_add(t_y[:, kk], t_y[:, kk], t_tmp[:, :])
                    if kk == KD - 1:
                        ins_.then_inc(S["s_vbc"], 1)
                    if n == DST - 1 and cc + 1 < NCH:
                        finalize(cc + 1, kk)
            for kk in range(KD):
                q = cc * KD + kk
                W(eng, "s_zd0" if q % 2 == 0 else "s_zd1", 16 * (q // 2))
                eng.scalar_tensor_tensor(t_z[:, q % 2], t_xbuf[:, cc % 2, kk], t_d1p[:, kk:kk + 1],
                                         t_y[:, kk], op0=ALU.mult, op1=ALU.add).then_inc(S["s_z"], 1)

        for fc in range(c.FC):
            W(eng, "s_stat", (fc + 1) * c.DD)
            eng.tensor_scalar_mul(t_mrow[:, 0:c.ZC], ps_stat_z[:, :], 1.0 / c.D)
            eng.tensor_scalar_mul(t_vrow[:, 0:c.ZC], ps_stat_q[:, :], 1.0 / c.D)
            eng.tensor_mul(t_rrow[:, 0:c.ZC], t_mrow[:, 0:c.ZC], t_mrow[:, 0:c.ZC])
            eng.tensor_sub(t_vrow[:, 0:c.ZC], t_vrow[:, 0:c.ZC], t_rrow[:, 0:c.ZC]).then_inc(S["s_str"], 1)
            W(eng, "s_rep1", fc + 1)
            for dd in range(c.DD):
                eng.tensor_sub(t_tmpz[:, :], t_zfull[:, fc % 2, dd], ps_m1)
                eng.tensor_mul(t_x1[:, dd, fc * c.ZC:(fc + 1) * c.ZC], t_tmpz[:, :], ps_r1).then_inc(S["s_x1"], 1)
        for m in range(c.MI):
            for half in range(HH):
                q = HH * m + half
                W(eng, "s_sil", q + 1)
                W(eng, "s_hd0" if q % 2 == 0 else "s_hd1", 16 * (q // 2))
                eng.scalar_tensor_tensor(t_hbuf[:, q % 2], ps_gt[q % 2], t_w3sc[:, m:m + 1],
                                         t_silu[:, q % 2], op0=ALU.mult, op1=ALU.mult).then_inc(S["s_h"], 1)
        def pass2(fc2x, do):
            fl = fc2x * c.DD + do
            if do == 0:
                W(eng, "s_rep2", fc2x + 1)
            W(eng, "s_z2l0" if fl % 2 == 0 else "s_z2l1", 16 * (fl // 2 + 1))
            W(eng, "s_out0" if fl % 2 == 0 else "s_out1", 16 * (fl // 2))
            eng.tensor_sub(t_tmp2[:, :], t_z2l[:, do % 2], ps_m2)
            eng.tensor_mul(t_outb[:, fl % 2], t_tmp2[:, :], ps_r2).then_inc(S["s_o"], 1)

        for fc2 in range(c.FC2):
            for do in range(c.DD):
                fl = fc2 * c.DD + do
                W(eng, "s_ffn2", fl + 1)
                W(eng, "s_x1l", 16 * (fc2 + 1))
                W(eng, "s_stat2", fl - 1)
                W(eng, "s_z2d0" if fl % 2 == 0 else "s_z2d1", 16 * (fl // 2))
                eng.scalar_tensor_tensor(t_z2z[:, fl % 2, 0, :], ps_o2[fl % 2], t_w2sc[:, do:do + 1],
                                         t_x1c[:, do, :], op0=ALU.mult, op1=ALU.add).then_inc(S["s_z2"], 1)
                if fc2 >= 1:
                    pass2(fc2 - 1, do)
            W(eng, "s_stat2", (fc2 + 1) * c.DD)
            eng.tensor_scalar_mul(t_mrow[:, 0:c.NC2], ps_st2_z[:, :], 1.0 / c.D)
            eng.tensor_scalar_mul(t_vrow[:, 0:c.NC2], ps_st2_q[:, :], 1.0 / c.D)
            eng.tensor_mul(t_rrow[:, 0:c.NC2], t_mrow[:, 0:c.NC2], t_mrow[:, 0:c.NC2])
            eng.tensor_sub(t_vrow[:, 0:c.NC2], t_vrow[:, 0:c.NC2], t_rrow[:, 0:c.NC2]).then_inc(S["s_str2"], 1)
        for do in range(c.DD):
            pass2(c.FC2 - 1, do)

    return nc


def host_layout(cfg, L0_attn_x, L0_attn_A_log, L0_attn_D, L0_attn_x_proj_w,
                L0_attn_dt_proj_w, L0_attn_dt_proj_b,
                L0_ffn_w1_q, L0_ffn_w1_s, L0_ffn_w3_q, L0_ffn_w3_s,
                L0_ffn_w2_q, L0_ffn_w2_s):
    c = cfg
    f32 = np.float32
    x = np.asarray(L0_attn_x, f32).reshape(c.TALL, c.D)
    xt = np.ascontiguousarray(x.T)
    xpt = np.ascontiguousarray(np.asarray(L0_attn_x_proj_w, f32).T)
    dtwt = np.ascontiguousarray(np.asarray(L0_attn_dt_proj_w, f32).T)
    Al = np.asarray(L0_attn_A_log, f32)
    Dv = np.asarray(L0_attn_D, f32)
    btv = np.asarray(L0_attn_dt_proj_b, f32)
    w1 = np.asarray(L0_ffn_w1_q, f32)
    w3 = np.asarray(L0_ffn_w3_q, f32)
    w2 = np.asarray(L0_ffn_w2_q, f32)
    w1tt = np.ascontiguousarray(w1.reshape(c.MI, P, c.DD, P).transpose(0, 3, 2, 1))
    w3tt = np.ascontiguousarray(w3.reshape(c.MI, P, c.DD, P).transpose(0, 3, 2, 1))
    w2tt = np.ascontiguousarray(w2.reshape(c.DD, P, c.MI, P).transpose(0, 3, 2, 1)).reshape(c.DD, P, c.MI * P)
    w1s = np.ascontiguousarray(np.asarray(L0_ffn_w1_s, f32).reshape(c.MI, P).T)
    w3s = np.ascontiguousarray(np.asarray(L0_ffn_w3_s, f32).reshape(c.MI, P).T)
    w2s = np.ascontiguousarray(np.asarray(L0_ffn_w2_s, f32).reshape(c.DD, P).T)

    in_maps = []
    for i in range(NCORES):
        sl = slice(i * c.DSL, (i + 1) * c.DSL)
        in_maps.append({
            "x_sl": np.ascontiguousarray(xt[sl]),
            "xproj": np.ascontiguousarray(xpt[sl]),
            "dtw": np.ascontiguousarray(dtwt[:, sl]),
            "a_sl": np.ascontiguousarray(Al[sl].reshape(c.KD, P, c.DST).transpose(1, 0, 2)),
            "dtb": np.ascontiguousarray(btv[sl].reshape(c.KD, P).T),
            "dsl": np.ascontiguousarray(Dv[sl].reshape(c.KD, P).T),
            "w1tt": w1tt, "w3tt": w3tt, "w2tt": w2tt,
            "w1sp": w1s, "w3sp": w3s, "w2sp": w2s,
        })
    return in_maps


def assemble(cfg, results):
    c = cfg
    out = np.empty((c.TALL, c.D), np.float32)
    for i in range(NCORES):
        out[i * c.TOK:(i + 1) * c.TOK, :] = np.asarray(results[i]["out"]).T
    return out.reshape(c.B, c.S, c.D)


def kernel(**inputs):
    cfg = full_cfg()
    in_maps = host_layout(cfg, **inputs)
    with allow_ici_all_to_all():
        nc = build_nc(cfg)
        res = run_bass_kernel_spmd(nc, in_maps, core_ids=list(range(NCORES)))
    return assemble(cfg, res.results)


# version 06 (final)
# speedup vs baseline: 1.1460x; 1.0586x over previous
import contextlib
import numpy as np
import concourse.bass as bass
import concourse.mybir as mybir
from concourse.bass_utils import run_bass_kernel_spmd
from concourse.aot_env import allow_ici_all_to_all

F32 = mybir.dt.float32
BF16 = mybir.dt.bfloat16
AF = mybir.ActivationFunctionType
ALU = mybir.AluOpType

NCORES = 8
P = 128


class Cfg:
    def __init__(self, D, B, S, I, Tc, ZC, NC2, HU):
        self.D, self.B, self.S, self.I = D, B, S, I
        self.R, self.DST = 128, 16
        self.Tc, self.ZC, self.NC2, self.HU = Tc, ZC, NC2, HU
        self.DSL = D // NCORES
        self.KD = self.DSL // P
        self.TALL = B * S
        self.TOK = self.TALL // NCORES
        self.NCH = self.TALL // Tc
        self.NPB = S // Tc
        self.CPD = self.TOK // Tc
        self.MI = I // P
        self.DD = D // P
        self.FC = self.TOK // ZC
        self.FC2 = self.TOK // NC2
        self.HH = self.TOK // HU
        self.NMM = 512
        assert self.DSL % P == 0 and S % Tc == 0 and self.TOK % Tc == 0
        assert self.TOK % ZC == 0 and self.TOK % NC2 == 0 and self.TOK % HU == 0
        assert I % P == 0 and D % P == 0 and ZC <= 512 and NC2 <= 512 and HU <= 1024


def full_cfg():
    return Cfg(D=4096, B=2, S=8192, I=11008, Tc=1024, ZC=128, NC2=512, HU=1024)


def _cdiv(a, b):
    return (a + b - 1) // b


def build_nc(cfg):
    c = cfg
    nc = bass.Bass("TRN2", target_bir_lowering=False, debug=False)

    x_sl = nc.dram_tensor("x_sl", [c.DSL, c.TALL], F32, kind="ExternalInput").ap()
    xproj = nc.dram_tensor("xproj", [c.DSL, 160], F32, kind="ExternalInput").ap()
    dtw = nc.dram_tensor("dtw", [c.R, c.DSL], F32, kind="ExternalInput").ap()
    a_sl = nc.dram_tensor("a_sl", [P, c.KD, c.DST], F32, kind="ExternalInput").ap()
    dtb = nc.dram_tensor("dtb", [P, c.KD], F32, kind="ExternalInput").ap()
    dsl = nc.dram_tensor("dsl", [P, c.KD], F32, kind="ExternalInput").ap()
    w1tt = nc.dram_tensor("w1tt", [c.MI, P, c.DD, P], F32, kind="ExternalInput").ap()
    w3tt = nc.dram_tensor("w3tt", [c.MI, P, c.DD, P], F32, kind="ExternalInput").ap()
    w2tt = nc.dram_tensor("w2tt", [c.DD, P, c.MI * P], F32, kind="ExternalInput").ap()
    w1sp = nc.dram_tensor("w1sp", [P, c.MI], F32, kind="ExternalInput").ap()
    w3sp = nc.dram_tensor("w3sp", [P, c.MI], F32, kind="ExternalInput").ap()
    w2sp = nc.dram_tensor("w2sp", [P, c.DD], F32, kind="ExternalInput").ap()
    out_e = nc.dram_tensor("out", [c.D, c.TOK], F32, kind="ExternalOutput").ap()

    xp_part = nc.dram_tensor("xp_part", [c.NCH, 160, c.Tc], F32).ap()
    xp_red = nc.dram_tensor("xp_red", [c.NCH, 160, c.Tc], F32).ap()
    a2a_in = nc.dram_tensor("a2a_in", [c.D, c.TOK], F32).ap()
    a2a_out = nc.dram_tensor("a2a_out", [c.D, c.TOK], F32).ap()
    h_dram = nc.dram_tensor("h_dram", [P, c.MI, c.TOK], BF16).ap()
    x1_dram = nc.dram_tensor("x1_dram", [P, c.DD, c.TOK], BF16).ap()
    z2_dram = nc.dram_tensor("z2_dram", [P, c.DD, c.NC2], F32).ap()
    w2bf_d = nc.dram_tensor("w2bf_d", [c.DD, P, c.MI * P], BF16).ap()
    z2_dram2 = nc.dram_tensor("z2_dram2", [P, c.DD, c.NC2], F32).ap()
    bcbf_d = nc.dram_tensor("bcbf_d", [c.NCH, 32, c.Tc], BF16).ap()

    stack = contextlib.ExitStack()
    nc._cleanup_stack = stack

    arena = None
    CAP = None
    for ai, cand in enumerate(range(nc.SBUF_PARTITION_SIZE_BYTES, 150000, -2048)):
        try:
            arena = stack.enter_context(nc.sbuf_tensor(f"arena{ai}", [P, cand // 4], F32))
            CAP = cand
            break
        except (AssertionError, ValueError):
            continue
    assert arena is not None

    class Region:
        def __init__(self, base):
            self.off = base

        def alloc(self, nbytes):
            nbytes = (nbytes + 31) & ~31
            o = self.off
            self.off += nbytes
            assert self.off <= CAP, f"arena overflow {self.off} > {CAP}"
            return o

    def carve(off, shape, dtype=F32, p0=0, pn=P):
        esz = 4 if dtype == F32 else 2
        n = 1
        for s in shape:
            n *= s
        words = (n * esz + 3) // 4
        ap = arena[p0:p0 + pn, off // 4: off // 4 + words]
        if dtype != F32:
            ap = ap.bitcast(dtype)[:, :n]
        if len(shape) > 1:
            letters = "abcdefg"[: len(shape)]
            spec = "p (" + " ".join(letters) + ") -> p " + " ".join(letters)
            kw = {letters[i]: shape[i] for i in range(len(shape))}
            ap = ap.rearrange(spec, **kw)
        return ap

    RW = max(c.ZC, c.NC2)
    per = Region(0)
    t_state = carve(per.alloc(c.KD * c.DST * 4), [c.KD, c.DST])
    t_araw = carve(per.alloc(c.KD * c.DST * 4), [c.KD, c.DST])
    t_aneg = carve(per.alloc(c.KD * c.DST * 4), [c.KD, c.DST])
    t_ta = carve(per.alloc(c.KD * c.DST * 4), [c.KD, c.DST])
    t_dtb = carve(per.alloc(c.KD * 4), [c.KD])
    t_dsl = carve(per.alloc(c.KD * 4), [c.KD])
    t_d1p = carve(per.alloc(c.KD * 4), [c.KD])
    t_xpw = carve(per.alloc(c.KD * 160 * 4), [c.KD, 160])
    t_dtw = carve(per.alloc(c.DSL * 4), [c.DSL])
    t_w1sc = carve(per.alloc(c.MI * 4), [c.MI])
    t_w3sc = carve(per.alloc(c.MI * 4), [c.MI])
    t_w2sc = carve(per.alloc(c.DD * 4), [c.DD])
    t_ones_col = carve(per.alloc(4), [1])
    t_ones_row = carve(per.alloc(P * 4), [P], p0=0, pn=1)
    t_cns = carve(per.alloc(4 * 4), [4])
    t_mrow = carve(per.alloc(RW * 4), [RW], p0=0, pn=1)
    t_vrow = carve(per.alloc(RW * 4), [RW], p0=0, pn=1)
    t_rrow = carve(per.alloc(RW * 4), [RW], p0=0, pn=1)
    PER_END = per.off

    CB = 1024
    p1 = Region(PER_END)
    t_xbuf = carve(p1.alloc(2 * c.KD * c.Tc * 4), [2, c.KD, c.Tc])
    t_dtin = carve(p1.alloc(2 * c.Tc * 4), [2, c.Tc])
    t_dtins = carve(p1.alloc(c.Tc * 4), [c.Tc])
    t_bcs = carve(p1.alloc(c.Tc * 4), [c.Tc], p0=0, pn=32)
    t_dt = carve(p1.alloc(c.KD * c.Tc * 4), [c.KD, c.Tc])
    t_u = carve(p1.alloc(c.KD * c.Tc * 2), [c.KD, c.Tc], BF16)
    t_y = carve(p1.alloc(c.KD * c.Tc * 4), [c.KD, c.Tc])
    t_t1 = carve(p1.alloc(c.KD * c.Tc * 4), [c.KD, c.Tc])
    t_t3 = carve(p1.alloc(c.KD * c.Tc * 4), [c.KD, c.Tc])
    t_t02 = carve(p1.alloc(c.Tc * 4), [c.Tc])
    t_dA = carve(p1.alloc(2 * c.Tc * 2), [2, c.Tc], BF16)
    t_dbx = carve(p1.alloc(c.Tc * 2), [c.Tc], BF16)
    t_h = carve(p1.alloc(c.Tc * 2), [c.Tc], BF16)
    t_tmp = carve(p1.alloc(c.Tc * 2), [c.Tc], BF16)
    t_brep = carve(p1.alloc(4 * c.Tc * 2), [4, c.Tc], BF16)
    t_crep = carve(p1.alloc(4 * c.Tc * 2), [4, c.Tc], BF16)
    t_z = carve(p1.alloc(2 * c.Tc * 4), [2, c.Tc])
    t_bcrd = carve(p1.alloc(c.Tc * 4), [c.Tc], p0=0, pn=32)
    t_bcbf = carve(p1.alloc(c.Tc * 2), [c.Tc], BF16, p0=0, pn=32)
    t_cst = carve(p1.alloc(2 * CB * 4), [2, CB])
    t_cbf = carve(p1.alloc(2 * CB * 2), [2, CB], BF16)
    assert p1.off <= CAP

    f1 = Region(PER_END)
    t_x1 = carve(f1.alloc(c.DD * c.TOK * 2), [c.DD, c.TOK], BF16)
    X1_END = f1.off
    t_zfull = carve(f1.alloc(2 * c.DD * c.ZC * 4), [2, c.DD, c.ZC])
    t_zsqp = carve(f1.alloc(4 * c.ZC * 4), [4, c.ZC])
    t_tmpz = carve(f1.alloc(c.ZC * 4), [c.ZC])
    assert f1.off <= CAP
    f2a = Region(X1_END)
    t_wst = carve(f2a.alloc(c.DD * P * 4), [c.DD * P])
    t_w1bf = carve(f2a.alloc(2 * c.DD * P * 2), [2, c.DD, P], BF16)
    t_w3bf = carve(f2a.alloc(2 * c.DD * P * 2), [2, c.DD, P], BF16)
    t_silu = carve(f2a.alloc(2 * c.HU * 4), [2, c.HU])
    t_hbuf = carve(f2a.alloc(2 * c.HU * 2), [2, c.HU], BF16)
    assert f2a.off <= CAP
    f2b = Region(PER_END)
    t_hbig = carve(f2b.alloc(c.MI * c.NC2 * 2), [c.MI, c.NC2], BF16)
    t_w2bf = carve(f2b.alloc(2 * c.MI * P * 2), [2, c.MI, P], BF16)
    t_x1c = carve(f2b.alloc(c.DD * c.NC2 * 2), [c.DD, c.NC2], BF16)
    t_z2z = carve(f2b.alloc(2 * 2 * c.NC2 * 4), [2, 2, c.NC2])
    t_z2l = carve(f2b.alloc(2 * c.NC2 * 4), [2, c.NC2])
    t_outb = carve(f2b.alloc(2 * c.NC2 * 4), [2, c.NC2])
    t_tmp2 = carve(f2b.alloc(c.NC2 * 4), [c.NC2])
    assert f2b.off <= CAP

    ps = stack.enter_context(nc.psum_tensor("ps", [P, 4096], F32))
    BK = 512
    ps_xp1 = ps[:, 0:c.Tc]
    ps_xp2 = ps[0:32, 2 * BK:2 * BK + c.Tc]
    ps_dt = [ps[:, 4 * BK + 2 * i * BK: 4 * BK + 2 * i * BK + c.Tc] for i in range(2)]
    ps_stat_z = ps[0:1, 0:c.ZC]
    ps_stat_q = ps[0:1, BK:BK + c.ZC]
    ps_m1 = ps[:, 2 * BK:2 * BK + c.ZC]
    ps_r1 = ps[:, 3 * BK:3 * BK + c.ZC]
    ps_up = [ps[:, 0:c.HU], ps[:, 2 * BK:2 * BK + c.HU]]
    ps_gt = [ps[:, 4 * BK:4 * BK + c.HU], ps[:, 6 * BK:6 * BK + c.HU]]
    ps_o2 = [ps[:, 0:c.NC2], ps[:, BK:BK + c.NC2]]
    ps_st2_z = ps[0:1, 2 * BK:2 * BK + c.NC2]
    ps_st2_q = ps[0:1, 3 * BK:3 * BK + c.NC2]
    ps_m2 = ps[:, 4 * BK:4 * BK + c.NC2]
    ps_r2 = ps[:, 5 * BK:5 * BK + c.NC2]

    semnames = [
        "s_x0", "s_x1b", "s_bc0", "s_bc1", "s_bc2", "s_bc3", "s_cbf", "s_gbs", "s_zd0", "s_zd1",
        "s_xp", "s_dt", "s_xpe", "s_sp", "s_dts",
        "s_da", "s_hts", "s_vbc", "s_z", "s_gw", "s_gr0", "s_gr1", "s_grb", "s_cc", "s_gc",
        "s_prc", "s_vpr",
        "s_wcl0", "s_wcl1", "s_wcv", "s_wcs0", "s_wcs1",
        "s_zl0", "s_zl1", "s_stat", "s_str", "s_st", "s_rep1", "s_x1", "s_x1d",
        "s_wl", "s_cv", "s_ffn1", "s_sil", "s_h", "s_hd0", "s_hd1",
        "s_hla", "s_hlb", "s_x1l", "s_w2a", "s_w2b", "s_ffn2", "s_z2", "s_sq1", "s_sq2",
        "s_stat2", "s_str2", "s_st2", "s_rep2", "s_z2d0", "s_z2d1",
        "s_z2l0", "s_z2l1", "s_o", "s_out0", "s_out1", "s_vch", "s_sch",
    ]
    S = {}
    for nm in semnames:
        S[nm] = stack.enter_context(nc.semaphore(nm))
    XS = ["s_x0", "s_x1b"]

    DST, KD, Tc, NCH = c.DST, c.KD, c.Tc, c.NCH
    NSUB = min(c.NMM, Tc)
    NT = Tc // NSUB
    HH = c.HH
    NHG = min(4, c.MI)
    mper = _cdiv(c.MI, NHG)
    NGR = _cdiv(c.MI, mper)
    NGC = 8

    conv_blocks = []
    nb2 = _cdiv(c.MI * P, CB)
    for do in range(c.DD):
        for q in range(nb2):
            lo, hi = q * CB, min((q + 1) * CB, c.MI * P)
            conv_blocks.append((w2tt[do][:, lo:hi], w2bf_d[do][:, lo:hi], hi - lo))
    NBLK = len(conv_blocks)
    NSLOT = NCH * DST
    base, extra = divmod(NBLK, NSLOT)
    slot_lo = [0]
    for s_ in range(NSLOT):
        slot_lo.append(slot_lo[-1] + base + (1 if s_ < extra else 0))

    x_r = x_sl.rearrange("(k p) t -> p k t", p=P)

    def W(eng, sem, v):
        if v > 0:
            eng.wait_ge(S[sem], v)

    class _Proxy:
        def __init__(self, ch, ins):
            self._ch, self._ins = ch, ins

        def then_inc(self, sem, val=1):
            ch = self._ch
            assert ch._pending is self._ins
            n = ch._counts.get(sem.name, 0) + val
            ch._counts[sem.name] = n
            self._ins.then_inc(sem, val)
            ch._last = (sem, n)
            ch._pending = None
            return self

    class _Chain:

        def __init__(self, e, sem):
            self._e, self._sem = e, sem
            self._counts = {}
            self._pending = None
            self._last = None

        def _flush(self):
            if self._pending is not None:
                n = self._counts.get(self._sem.name, 0) + 1
                self._counts[self._sem.name] = n
                self._pending.then_inc(self._sem, 1)
                self._last = (self._sem, n)
                self._pending = None

        def wait_ge(self, sem, v):
            self._flush()
            self._e.wait_ge(sem, v)

        def __getattr__(self, name):
            f = getattr(self._e, name)
            ch = self

            def wrapped(*a, **k):
                ch._flush()
                if ch._last is not None:
                    ch._e.wait_ge(ch._last[0], ch._last[1])
                ins = f(*a, **k)
                ch._pending = ins
                return _Proxy(ch, ins)
            return wrapped

    blk = stack.enter_context(nc.Block())

    @blk.sync
    def _(eng):
        for i in range(min(2, NCH)):
            eng.dma_start(out=t_xbuf[:, i % 2], in_=x_r[:, :, i * Tc:(i + 1) * Tc]).then_inc(S[XS[i % 2]], 16)
        for cc in range(NCH):
            for n in range(DST):
                flat = cc * DST + n
                W(eng, "s_vbc", flat - 3)
                W(eng, "s_gbs", 16 * (cc + 1))
                src_b = bcbf_d[cc, n: n + 1, :].broadcast_to([P, Tc])
                src_c = bcbf_d[cc, DST + n: DST + n + 1, :].broadcast_to([P, Tc])
                bs = f"s_bc{n % 4}"
                eng.dma_start(out=t_brep[:, n % 4], in_=src_b).then_inc(S[bs], 16)
                eng.dma_start(out=t_crep[:, n % 4], in_=src_c).then_inc(S[bs], 16)
                for b in range(slot_lo[flat], slot_lo[flat + 1]):
                    W(eng, "s_wcv", b - 1)
                    srcap, dstap, w_ = conv_blocks[b]
                    eng.dma_start(out=t_cst[:, b % 2][:, 0:w_],
                                  in_=srcap).then_inc(S["s_wcl0" if b % 2 == 0 else "s_wcl1"], 16)
            j = cc // c.CPD
            coff = (cc % c.CPD) * Tc
            for kk in range(KD):
                q = cc * KD + kk
                W(eng, "s_z", q + 1)
                r0 = j * c.DSL + kk * P
                eng.dma_start(out=a2a_in[r0:r0 + P, coff:coff + Tc],
                              in_=t_z[:, q % 2]).then_inc(S["s_zd0" if q % 2 == 0 else "s_zd1"], 16)
            if cc + 2 < NCH:
                W(eng, "s_z", KD * (cc + 1))
                i = cc + 2
                eng.dma_start(out=t_xbuf[:, i % 2], in_=x_r[:, :, i * Tc:(i + 1) * Tc]).then_inc(S[XS[i % 2]], 16)

        a2o = a2a_out.rearrange("(dd p) t -> p dd t", p=P)
        W(eng, "s_cc", NCH + 1)
        for fc in range(c.FC):
            W(eng, "s_x1", (fc - 1) * c.DD)
            hd = c.DD // 2
            for g in range(2):
                eng.dma_start(out=t_zfull[:, fc % 2, g * hd:(g + 1) * hd],
                              in_=a2o[:, g * hd:(g + 1) * hd, fc * c.ZC:(fc + 1) * c.ZC]).then_inc(
                                  S["s_zl0" if fc % 2 == 0 else "s_zl1"], 16)
        for fc in range(c.FC):
            for dd in range(c.DD):
                W(eng, "s_x1", fc * c.DD + dd + 1)
                eng.dma_start(out=x1_dram[:, dd, fc * c.ZC:(fc + 1) * c.ZC],
                              in_=t_x1[:, dd, fc * c.ZC:(fc + 1) * c.ZC]).then_inc(S["s_x1d"], 16)
        W(eng, "s_x1", c.FC * c.DD)
        for m in range(c.MI):
            W(eng, "s_cv", 2 * m)
            eng.dma_start(out=t_wst[:, :], in_=w1tt[m].rearrange("p k m -> p (k m)")).then_inc(S["s_wl"], 16)
            W(eng, "s_cv", 2 * m + 1)
            eng.dma_start(out=t_wst[:, :], in_=w3tt[m].rearrange("p k m -> p (k m)")).then_inc(S["s_wl"], 16)
        nqh = HH * c.MI
        W(eng, "s_hd0", 16 * ((nqh + 1) // 2))
        W(eng, "s_hd1", 16 * (nqh // 2))
        W(eng, "s_x1d", 16 * c.FC * c.DD)
        for fc2 in range(c.FC2):
            W(eng, "s_ffn2", c.DD * fc2)
            for g in range(NGR):
                m0, m1_ = g * mper, min((g + 1) * mper, c.MI)
                eng.dma_start(out=t_hbig[:, m0:m1_, :],
                              in_=h_dram[:, m0:m1_, fc2 * c.NC2:(fc2 + 1) * c.NC2]).then_inc(
                                  S["s_hla" if g < (NGR + 1) // 2 else "s_hlb"], 16)
            W(eng, "s_z2", c.DD * fc2)
            eng.dma_start(out=t_x1c[:, :, :],
                          in_=x1_dram[:, :, fc2 * c.NC2:(fc2 + 1) * c.NC2]).then_inc(S["s_x1l"], 16)
            for do in range(c.DD):
                fl = fc2 * c.DD + do
                W(eng, "s_ffn2", fl - 1)
                eng.dma_start(out=t_w2bf[:, fl % 2].rearrange("p a b -> p (a b)"),
                              in_=w2bf_d[do]).then_inc(S["s_w2a" if fl % 2 == 0 else "s_w2b"], 16)

    @blk.gpsimd
    def _(eng):
        eng.dma_start(out=t_xpw[:, :, :], in_=xproj.rearrange("(k p) f -> p k f", p=P)).then_inc(S["s_gc"], 16)
        eng.dma_start(out=t_dtw[:, :], in_=dtw).then_inc(S["s_gc"], 16)
        eng.dma_start(out=t_araw[:, :, :], in_=a_sl).then_inc(S["s_gc"], 16)
        eng.dma_start(out=t_dtb[:, :], in_=dtb).then_inc(S["s_gc"], 16)
        eng.dma_start(out=t_dsl[:, :], in_=dsl).then_inc(S["s_gc"], 16)
        eng.dma_start(out=t_w1sc[:, :], in_=w1sp).then_inc(S["s_gc"], 16)
        eng.dma_start(out=t_w3sc[:, :], in_=w3sp).then_inc(S["s_gc"], 16)
        eng.dma_start(out=t_w2sc[:, :], in_=w2sp).then_inc(S["s_gc"], 16)
        for cc in range(NCH):
            W(eng, "s_xpe", cc + 1)
            eng.dma_start(out=xp_part[cc, 0:c.R, :], in_=t_dtins[:, :]).then_inc(S["s_gw"], 16)
            eng.dma_start(out=xp_part[cc, c.R:160, :], in_=t_bcs[:, :]).then_inc(S["s_gw"], 16)
            eng.wait_ge(S["s_gw"], 32 * (cc + 1))
            eng.collective_compute(
                "AllReduce", ALU.add, replica_groups=[list(range(NCORES))],
                ins=[xp_part[cc].opt()], outs=[xp_red[cc].opt()],
            ).then_inc(S["s_cc"], 1)
            eng.wait_ge(S["s_cc"], cc + 1)
            W(eng, "s_dt", KD * (cc - 1))
            eng.dma_start(out=t_dtin[:, cc % 2],
                          in_=xp_red[cc, 0:c.R, :]).then_inc(S["s_gr0" if cc % 2 == 0 else "s_gr1"], 16)
            W(eng, "s_cbf", cc)
            eng.dma_start(out=t_bcrd[:, :], in_=xp_red[cc, c.R:160, :]).then_inc(S["s_grb"], 16)
            if cc > 0:
                for b in range(slot_lo[(cc - 1) * DST], slot_lo[cc * DST]):
                    W(eng, "s_wcv", b + 1)
                    srcap, dstap, w_ = conv_blocks[b]
                    eng.dma_start(out=dstap, in_=t_cbf[:, b % 2][:, 0:w_]).then_inc(
                        S["s_wcs0" if b % 2 == 0 else "s_wcs1"], 16)
            W(eng, "s_cbf", cc + 1)
            eng.dma_start(out=bcbf_d[cc], in_=t_bcbf[:, :]).then_inc(S["s_gbs"], 16)
        for b in range(slot_lo[(NCH - 1) * DST], slot_lo[NCH * DST]):
            W(eng, "s_wcv", b + 1)
            srcap, dstap, w_ = conv_blocks[b]
            eng.dma_start(out=dstap, in_=t_cbf[:, b % 2][:, 0:w_]).then_inc(
                S["s_wcs0" if b % 2 == 0 else "s_wcs1"], 16)
        eng.wait_ge(S["s_wcs0"], 16 * ((NBLK + 1) // 2))
        if NBLK // 2 > 0:
            eng.wait_ge(S["s_wcs1"], 16 * (NBLK // 2))
        nq = KD * NCH
        eng.wait_ge(S["s_zd0"], 16 * ((nq + 1) // 2))
        if nq // 2 > 0:
            eng.wait_ge(S["s_zd1"], 16 * (nq // 2))
        eng.wait_ge(S["s_gr0"], 16 * ((NCH + 1) // 2))
        if NCH // 2 > 0:
            eng.wait_ge(S["s_gr1"], 16 * (NCH // 2))
        eng.collective_compute(
            "AllToAll", ALU.bypass, replica_groups=[list(range(NCORES))],
            ins=[a2a_in.opt()], outs=[a2a_out.opt()],
        ).then_inc(S["s_cc"], 1)

        def cvm(mc):
            W(eng, "s_wl", 16 * (2 * mc + 1))
            W(eng, "s_ffn1", HH * (mc - 1))
            eng.tensor_copy(t_w1bf[:, mc % 2].rearrange("p a b -> p (a b)"), t_wst[:, :]).then_inc(S["s_cv"], 1)
            W(eng, "s_wl", 16 * (2 * mc + 2))
            eng.tensor_copy(t_w3bf[:, mc % 2].rearrange("p a b -> p (a b)"), t_wst[:, :]).then_inc(S["s_cv"], 1)

        def hstores(m):
            for half in range(HH):
                q = HH * m + half
                W(eng, "s_h", q + 1)
                eng.dma_start(out=h_dram[:, m, half * c.HU:(half + 1) * c.HU],
                              in_=t_hbuf[:, q % 2]).then_inc(S["s_hd0" if q % 2 == 0 else "s_hd1"], 16)

        cvm(0)
        if c.MI > 1:
            cvm(1)
        for m in range(c.MI):
            if m >= 1:
                hstores(m - 1)
            if m + 2 < c.MI:
                cvm(m + 2)
        hstores(c.MI - 1)
        z2d2 = [z2_dram, z2_dram2]

        def p2load(fc2x, do):
            fl = fc2x * c.DD + do
            W(eng, "s_o", fl - 1)
            W(eng, "s_z2d0" if fl % 2 == 0 else "s_z2d1", 16 * (fl // 2 + 1))
            eng.dma_start(out=t_z2l[:, do % 2],
                          in_=z2d2[fc2x % 2][:, do, :]).then_inc(S["s_z2l0" if fl % 2 == 0 else "s_z2l1"], 16)
            W(eng, "s_o", fl + 1)
            eng.dma_start(out=out_e[do * P:(do + 1) * P, fc2x * c.NC2:(fc2x + 1) * c.NC2],
                          in_=t_outb[:, fl % 2]).then_inc(S["s_out0" if fl % 2 == 0 else "s_out1"], 16)

        for fc2 in range(c.FC2):
            for do in range(c.DD):
                fl = fc2 * c.DD + do
                W(eng, "s_z2", fl + 1)
                W(eng, "s_o", (fc2 - 1) * c.DD)
                eng.dma_start(out=z2d2[fc2 % 2][:, do, :],
                              in_=t_z2z[:, fl % 2, 0, :]).then_inc(S["s_z2d0" if fl % 2 == 0 else "s_z2d1"], 16)
                if fc2 >= 1:
                    p2load(fc2 - 1, do)
        for do in range(c.DD):
            p2load(c.FC2 - 1, do)
        nfl = c.FC2 * c.DD
        eng.wait_ge(S["s_out0"], 16 * ((nfl + 1) // 2))
        if nfl // 2 > 0:
            eng.wait_ge(S["s_out1"], 16 * (nfl // 2))

    @blk.tensor
    def _(eng):
        def xp_mm(cc):
            last = None
            for kk in range(KD):
                for j in range(NT):
                    sl = slice(j * NSUB, (j + 1) * NSUB)
                    eng.matmul(ps_xp1[:, sl], t_xpw[:, kk, 0:P],
                               t_xbuf[:, cc % 2, kk, sl],
                               start=(kk == 0), stop=(kk == KD - 1))
                    last = eng.matmul(ps_xp2[:, sl], t_xpw[:, kk, P:160],
                                      t_xbuf[:, cc % 2, kk, sl],
                                      start=(kk == 0), stop=(kk == KD - 1), skip_group_check=True)
            last.then_inc(S["s_xp"], 1)

        eng.wait_ge(S["s_gc"], 16 * NGC)
        eng.wait_ge(S[XS[0]], 16)
        xp_mm(0)
        for cc in range(NCH):
            W(eng, "s_gr0" if cc % 2 == 0 else "s_gr1", 16 * (cc // 2 + 1))
            for kk in range(KD):
                W(eng, "s_sp", cc * KD + kk - 1)
                last = None
                for j in range(NT):
                    sl = slice(j * NSUB, (j + 1) * NSUB)
                    last = eng.matmul(ps_dt[kk % 2][:, sl], t_dtw[:, kk * P:(kk + 1) * P],
                                      t_dtin[:, cc % 2, sl], start=True, stop=True,
                                      skip_group_check=True)
                last.then_inc(S["s_dt"], 1)
            if cc + 1 < NCH:
                i = cc + 1
                W(eng, XS[i % 2], 16 * (i // 2 + 1))
                W(eng, "s_xpe", cc + 1)
                xp_mm(cc + 1)

        W(eng, "s_vpr", 1)
        for fc in range(c.FC):
            for dd in range(c.DD):
                fl = fc * c.DD + dd
                if dd == 0:
                    W(eng, "s_zl0" if fc % 2 == 0 else "s_zl1", 16 * 2 * (fc // 2 + 1))
                    W(eng, "s_str", fc)
                W(eng, "s_sq1", fl + 1)
                eng.matmul(ps_stat_z[:, :], t_ones_col[:, :], t_zfull[:, fc % 2, dd],
                           start=(dd == 0), stop=(dd == c.DD - 1), skip_group_check=True)
                eng.matmul(ps_stat_q[:, :], t_ones_col[:, :], t_zsqp[:, fl % 4],
                           start=(dd == 0), stop=(dd == c.DD - 1),
                           skip_group_check=True).then_inc(S["s_stat"], 1)
            W(eng, "s_st", fc + 1)
            W(eng, "s_x1", fc * c.DD)
            eng.matmul(ps_m1[:, :], t_ones_row[:, :], t_mrow[:, 0:c.ZC], start=True, stop=True,
                       skip_group_check=True)
            eng.matmul(ps_r1[:, :], t_ones_row[:, :], t_rrow[:, 0:c.ZC], start=True, stop=True,
                       skip_group_check=True).then_inc(S["s_rep1"], 1)

        W(eng, "s_x1", c.FC * c.DD)
        nun = c.HU // NSUB if c.HU >= NSUB else 1
        nsub1 = min(NSUB, c.HU)
        for m in range(c.MI):
            W(eng, "s_cv", 2 * (m + 1))
            for half in range(HH):
                q = HH * m + half
                par = q % 2
                W(eng, "s_sil", q - 1)
                W(eng, "s_h", q - 1)
                last = None
                for j in range(nun):
                    toff = half * c.HU + j * nsub1
                    sl = slice(j * nsub1, (j + 1) * nsub1)
                    for kk in range(c.DD):
                        eng.matmul(ps_up[par][:, sl], t_w1bf[:, m % 2, kk],
                                   t_x1[:, kk, toff:toff + nsub1],
                                   start=(kk == 0), stop=(kk == c.DD - 1), skip_group_check=True)
                    for kk in range(c.DD):
                        last = eng.matmul(ps_gt[par][:, sl], t_w3bf[:, m % 2, kk],
                                          t_x1[:, kk, toff:toff + nsub1],
                                          start=(kk == 0), stop=(kk == c.DD - 1),
                                          skip_group_check=True)
                last.then_inc(S["s_ffn1"], 1)

        W(eng, "s_h", HH * c.MI)
        W(eng, "s_sil", HH * c.MI)
        for fc2 in range(c.FC2):
            def stats2_mm(fl):
                do_ = fl % c.DD
                W(eng, "s_sq2", fl + 1)
                if do_ == 0:
                    W(eng, "s_str2", fc2)
                eng.matmul(ps_st2_z[:, :], t_ones_col[:, :], t_z2z[:, fl % 2, 0, :],
                           start=(do_ == 0), stop=(do_ == c.DD - 1), skip_group_check=True)
                eng.matmul(ps_st2_q[:, :], t_ones_col[:, :], t_z2z[:, fl % 2, 1, :],
                           start=(do_ == 0), stop=(do_ == c.DD - 1),
                           skip_group_check=True).then_inc(S["s_stat2"], 1)

            for do in range(c.DD):
                fl = fc2 * c.DD + do
                W(eng, "s_w2a" if fl % 2 == 0 else "s_w2b", 16 * (fl // 2 + 1))
                W(eng, "s_z2", fl - 1)
                nga = (NGR + 1) // 2
                last = None
                for mk in range(c.MI):
                    if do == 0 and mk == 0:
                        W(eng, "s_hla", 16 * nga * (fc2 + 1))
                    if do == 0 and mk == nga * mper and NGR > nga:
                        W(eng, "s_hlb", 16 * (NGR - nga) * (fc2 + 1))
                    last = eng.matmul(ps_o2[fl % 2][:, :], t_w2bf[:, fl % 2, mk], t_hbig[:, mk, :],
                                      start=(mk == 0), stop=(mk == c.MI - 1), skip_group_check=True)
                last.then_inc(S["s_ffn2"], 1)
                if do >= 2:
                    stats2_mm(fl - 2)
            stats2_mm(fc2 * c.DD + c.DD - 2)
            stats2_mm(fc2 * c.DD + c.DD - 1)
            W(eng, "s_st2", fc2 + 1)
            W(eng, "s_o", fc2 * c.DD)
            eng.matmul(ps_m2[:, :], t_ones_row[:, :], t_mrow[:, 0:c.NC2], start=True, stop=True,
                       skip_group_check=True)
            eng.matmul(ps_r2[:, :], t_ones_row[:, :], t_rrow[:, 0:c.NC2], start=True, stop=True,
                       skip_group_check=True).then_inc(S["s_rep2"], 1)

    @blk.scalar
    def _(eng):
        eng = _Chain(eng, S["s_sch"])
        eng.wait_ge(S["s_gc"], 16 * NGC)
        eng.activation(t_ta[:, :, :], t_araw[:, :, :], AF.Exp).then_inc(S["s_prc"], 1)
        W(eng, "s_vpr", 1)

        def softplus(cc):
            for kk in range(KD):
                W(eng, "s_dt", cc * KD + kk + 1)
                W(eng, "s_dts", (cc - 1) * KD + kk + 1)
                b = t_dtb[:, kk:kk + 1]
                eng.activation(t_t02[:, :], ps_dt[kk % 2], AF.Abs, bias=b)
                eng.activation(t_t1[:, kk], ps_dt[kk % 2], AF.Relu, bias=b)
                eng.activation(t_t02[:, :], t_t02[:, :], AF.Exp, scale=t_cns[:, 0:1])
                eng.activation(t_t3[:, kk], t_t02[:, :], AF.Ln, bias=t_cns[:, 1:2]).then_inc(S["s_sp"], 1)

        W(eng, "s_grb", 16)
        eng.copy(t_bcbf[:, :], t_bcrd[:, :]).then_inc(S["s_cbf"], 1)
        softplus(0)
        for cc in range(NCH):
            for n in range(DST):
                for kk in range(KD):
                    flat = (cc * DST + n) * KD + kk
                    W(eng, "s_hts", flat - 1)
                    W(eng, "s_dts", cc * KD + kk + 1)
                    eng.activation(t_dA[:, flat % 2], t_dt[:, kk], AF.Exp,
                                   scale=t_aneg[:, kk, n:n + 1]).then_inc(S["s_da"], 1)
                flat_s = cc * DST + n
                for b in range(slot_lo[flat_s], slot_lo[flat_s + 1]):
                    W(eng, "s_wcl0" if b % 2 == 0 else "s_wcl1", 16 * (b // 2 + 1))
                    W(eng, "s_wcs0" if b % 2 == 0 else "s_wcs1", 16 * (b // 2))
                    w_ = conv_blocks[b][2]
                    eng.copy(t_cbf[:, b % 2][:, 0:w_], t_cst[:, b % 2][:, 0:w_]).then_inc(S["s_wcv"], 1)
                if n == 7 and cc + 1 < NCH:
                    softplus(cc + 1)
                if n == 3 and cc + 1 < NCH:
                    W(eng, "s_grb", 16 * (cc + 2))
                    W(eng, "s_gbs", 16 * (cc + 1))
                    eng.copy(t_bcbf[:, :], t_bcrd[:, :]).then_inc(S["s_cbf"], 1)

        for fc in range(c.FC):
            for dd in range(c.DD):
                fl = fc * c.DD + dd
                if dd == 0:
                    W(eng, "s_zl0" if fc % 2 == 0 else "s_zl1", 16 * 2 * (fc // 2 + 1))
                W(eng, "s_stat", fl - 3)
                eng.activation(t_zsqp[:, fl % 4], t_zfull[:, fc % 2, dd], AF.Square).then_inc(S["s_sq1"], 1)
            W(eng, "s_str", fc + 1)
            eng.activation(t_rrow[:, 0:c.ZC], t_vrow[:, 0:c.ZC], AF.Ln, bias=t_cns[0:1, 2:3])
            eng.activation(t_rrow[:, 0:c.ZC], t_rrow[:, 0:c.ZC], AF.Exp, scale=t_cns[0:1, 3:4]).then_inc(S["s_st"], 1)
        for m in range(c.MI):
            for half in range(HH):
                q = HH * m + half
                W(eng, "s_ffn1", q + 1)
                W(eng, "s_h", q - 1)
                eng.activation(t_silu[:, q % 2], ps_up[q % 2], AF.Silu,
                               scale=t_w1sc[:, m:m + 1]).then_inc(S["s_sil"], 1)
        for fc2 in range(c.FC2):
            def z2sq(fl):
                W(eng, "s_z2", fl + 1)
                W(eng, "s_stat2", fl - 1)
                eng.activation(t_z2z[:, fl % 2, 1, :], t_z2z[:, fl % 2, 0, :], AF.Square).then_inc(S["s_sq2"], 1)

            for do in range(c.DD):
                fl = fc2 * c.DD + do
                if do >= 2:
                    z2sq(fl - 2)
            z2sq(fc2 * c.DD + c.DD - 2)
            z2sq(fc2 * c.DD + c.DD - 1)
            W(eng, "s_str2", fc2 + 1)
            eng.activation(t_rrow[:, 0:c.NC2], t_vrow[:, 0:c.NC2], AF.Ln, bias=t_cns[0:1, 2:3])
            eng.activation(t_rrow[:, 0:c.NC2], t_rrow[:, 0:c.NC2], AF.Exp,
                           scale=t_cns[0:1, 3:4]).then_inc(S["s_st2"], 1)

    @blk.vector
    def _(eng):
        eng = _Chain(eng, S["s_vch"])
        eng.memset(t_ones_col[:, :], 1.0)
        eng.memset(t_ones_row[:, :], 1.0)
        eng.memset(t_cns[:, 0:1], -1.0)
        eng.memset(t_cns[:, 1:2], 1.0)
        eng.memset(t_cns[:, 2:3], 1e-5)
        eng.memset(t_cns[:, 3:4], -0.5).then_inc(S["s_vpr"], 1)
        eng.wait_ge(S["s_gc"], 16 * NGC)
        W(eng, "s_prc", 1)
        eng.tensor_scalar_mul(t_aneg[:, :, :], t_ta[:, :, :], -1.0)
        eng.tensor_scalar_add(t_d1p[:, :], t_dsl[:, :], 1.0)

        eng.wait_ge(S["s_xp"], 1)
        eng.tensor_copy(t_dtins[:, :], ps_xp1)
        eng.tensor_copy(t_bcs[:, :], ps_xp2).then_inc(S["s_xpe"], 1)

        def finalize(cc2, kk):
            W(eng, "s_sp", cc2 * KD + kk + 1)
            eng.tensor_add(t_dt[:, kk], t_t1[:, kk], t_t3[:, kk])
            eng.tensor_mul(t_u[:, kk], t_dt[:, kk], t_xbuf[:, cc2 % 2, kk]).then_inc(S["s_dts"], 1)

        for kk in range(KD):
            finalize(0, kk)
        for cc in range(NCH):
            for n in range(DST):
                if n == 1 and cc + 1 < NCH:
                    W(eng, "s_xp", cc + 2)
                    W(eng, "s_gw", 32 * (cc + 1))
                    eng.tensor_copy(t_dtins[:, :], ps_xp1)
                    eng.tensor_copy(t_bcs[:, :], ps_xp2).then_inc(S["s_xpe"], 1)
                for kk in range(KD):
                    flat = (cc * DST + n) * KD + kk
                    W(eng, "s_da", flat + 1)
                    nn = cc * DST + n
                    W(eng, f"s_bc{n % 4}", 32 * (nn // 4 + 1))
                    eng.tensor_mul(t_dbx[:, :], t_u[:, kk], t_brep[:, n % 4])
                    init = 0.0 if (cc % c.NPB == 0) else t_state[:, kk, n:n + 1]
                    eng.tensor_tensor_scan(t_h[:, :], t_dA[:, flat % 2], t_dbx[:, :],
                                           init, op0=ALU.mult, op1=ALU.add).then_inc(S["s_hts"], 1)
                    eng.tensor_copy(t_state[:, kk, n:n + 1], t_h[:, Tc - 1:Tc])
                    if n == 0:
                        ins_ = eng.tensor_mul(t_y[:, kk], t_h[:, :], t_crep[:, n % 4])
                    else:
                        eng.tensor_mul(t_tmp[:, :], t_h[:, :], t_crep[:, n % 4])
                        ins_ = eng.tensor_add(t_y[:, kk], t_y[:, kk], t_tmp[:, :])
                    if kk == KD - 1:
                        ins_.then_inc(S["s_vbc"], 1)
                    if n == DST - 1 and cc + 1 < NCH:
                        finalize(cc + 1, kk)
            for kk in range(KD):
                q = cc * KD + kk
                W(eng, "s_zd0" if q % 2 == 0 else "s_zd1", 16 * (q // 2))
                eng.scalar_tensor_tensor(t_z[:, q % 2], t_xbuf[:, cc % 2, kk], t_d1p[:, kk:kk + 1],
                                         t_y[:, kk], op0=ALU.mult, op1=ALU.add).then_inc(S["s_z"], 1)

        for fc in range(c.FC):
            W(eng, "s_stat", (fc + 1) * c.DD)
            eng.tensor_scalar_mul(t_mrow[:, 0:c.ZC], ps_stat_z[:, :], 1.0 / c.D)
            eng.tensor_scalar_mul(t_vrow[:, 0:c.ZC], ps_stat_q[:, :], 1.0 / c.D)
            eng.tensor_mul(t_rrow[:, 0:c.ZC], t_mrow[:, 0:c.ZC], t_mrow[:, 0:c.ZC])
            eng.tensor_sub(t_vrow[:, 0:c.ZC], t_vrow[:, 0:c.ZC], t_rrow[:, 0:c.ZC]).then_inc(S["s_str"], 1)
            W(eng, "s_rep1", fc + 1)
            for dd in range(c.DD):
                eng.tensor_sub(t_tmpz[:, :], t_zfull[:, fc % 2, dd], ps_m1)
                eng.tensor_mul(t_x1[:, dd, fc * c.ZC:(fc + 1) * c.ZC], t_tmpz[:, :], ps_r1).then_inc(S["s_x1"], 1)
        for m in range(c.MI):
            for half in range(HH):
                q = HH * m + half
                W(eng, "s_sil", q + 1)
                W(eng, "s_hd0" if q % 2 == 0 else "s_hd1", 16 * (q // 2))
                eng.scalar_tensor_tensor(t_hbuf[:, q % 2], ps_gt[q % 2], t_w3sc[:, m:m + 1],
                                         t_silu[:, q % 2], op0=ALU.mult, op1=ALU.mult).then_inc(S["s_h"], 1)
        def pass2(fc2x, do):
            fl = fc2x * c.DD + do
            if do == 0:
                W(eng, "s_rep2", fc2x + 1)
            W(eng, "s_z2l0" if fl % 2 == 0 else "s_z2l1", 16 * (fl // 2 + 1))
            W(eng, "s_out0" if fl % 2 == 0 else "s_out1", 16 * (fl // 2))
            eng.tensor_sub(t_tmp2[:, :], t_z2l[:, do % 2], ps_m2)
            eng.tensor_mul(t_outb[:, fl % 2], t_tmp2[:, :], ps_r2).then_inc(S["s_o"], 1)

        for fc2 in range(c.FC2):
            for do in range(c.DD):
                fl = fc2 * c.DD + do
                W(eng, "s_ffn2", fl + 1)
                W(eng, "s_x1l", 16 * (fc2 + 1))
                W(eng, "s_stat2", fl - 1)
                W(eng, "s_z2d0" if fl % 2 == 0 else "s_z2d1", 16 * (fl // 2))
                eng.scalar_tensor_tensor(t_z2z[:, fl % 2, 0, :], ps_o2[fl % 2], t_w2sc[:, do:do + 1],
                                         t_x1c[:, do, :], op0=ALU.mult, op1=ALU.add).then_inc(S["s_z2"], 1)
                if fc2 >= 1:
                    pass2(fc2 - 1, do)
            W(eng, "s_stat2", (fc2 + 1) * c.DD)
            eng.tensor_scalar_mul(t_mrow[:, 0:c.NC2], ps_st2_z[:, :], 1.0 / c.D)
            eng.tensor_scalar_mul(t_vrow[:, 0:c.NC2], ps_st2_q[:, :], 1.0 / c.D)
            eng.tensor_mul(t_rrow[:, 0:c.NC2], t_mrow[:, 0:c.NC2], t_mrow[:, 0:c.NC2])
            eng.tensor_sub(t_vrow[:, 0:c.NC2], t_vrow[:, 0:c.NC2], t_rrow[:, 0:c.NC2]).then_inc(S["s_str2"], 1)
        for do in range(c.DD):
            pass2(c.FC2 - 1, do)

    return nc


def host_layout(cfg, L0_attn_x, L0_attn_A_log, L0_attn_D, L0_attn_x_proj_w,
                L0_attn_dt_proj_w, L0_attn_dt_proj_b,
                L0_ffn_w1_q, L0_ffn_w1_s, L0_ffn_w3_q, L0_ffn_w3_s,
                L0_ffn_w2_q, L0_ffn_w2_s):
    c = cfg
    f32 = np.float32
    x = np.asarray(L0_attn_x, f32).reshape(c.TALL, c.D)
    xt = np.ascontiguousarray(x.T)
    xpt = np.ascontiguousarray(np.asarray(L0_attn_x_proj_w, f32).T)
    dtwt = np.ascontiguousarray(np.asarray(L0_attn_dt_proj_w, f32).T)
    Al = np.asarray(L0_attn_A_log, f32)
    Dv = np.asarray(L0_attn_D, f32)
    btv = np.asarray(L0_attn_dt_proj_b, f32)
    w1 = np.asarray(L0_ffn_w1_q, f32)
    w3 = np.asarray(L0_ffn_w3_q, f32)
    w2 = np.asarray(L0_ffn_w2_q, f32)
    w1tt = np.ascontiguousarray(w1.reshape(c.MI, P, c.DD, P).transpose(0, 3, 2, 1))
    w3tt = np.ascontiguousarray(w3.reshape(c.MI, P, c.DD, P).transpose(0, 3, 2, 1))
    w2tt = np.ascontiguousarray(w2.reshape(c.DD, P, c.MI, P).transpose(0, 3, 2, 1)).reshape(c.DD, P, c.MI * P)
    w1s = np.ascontiguousarray(np.asarray(L0_ffn_w1_s, f32).reshape(c.MI, P).T)
    w3s = np.ascontiguousarray(np.asarray(L0_ffn_w3_s, f32).reshape(c.MI, P).T)
    w2s = np.ascontiguousarray(np.asarray(L0_ffn_w2_s, f32).reshape(c.DD, P).T)

    in_maps = []
    for i in range(NCORES):
        sl = slice(i * c.DSL, (i + 1) * c.DSL)
        in_maps.append({
            "x_sl": np.ascontiguousarray(xt[sl]),
            "xproj": np.ascontiguousarray(xpt[sl]),
            "dtw": np.ascontiguousarray(dtwt[:, sl]),
            "a_sl": np.ascontiguousarray(Al[sl].reshape(c.KD, P, c.DST).transpose(1, 0, 2)),
            "dtb": np.ascontiguousarray(btv[sl].reshape(c.KD, P).T),
            "dsl": np.ascontiguousarray(Dv[sl].reshape(c.KD, P).T),
            "w1tt": w1tt, "w3tt": w3tt, "w2tt": w2tt,
            "w1sp": w1s, "w3sp": w3s, "w2sp": w2s,
        })
    return in_maps


def assemble(cfg, results):
    c = cfg
    out = np.empty((c.TALL, c.D), np.float32)
    for i in range(NCORES):
        out[i * c.TOK:(i + 1) * c.TOK, :] = np.asarray(results[i]["out"]).T
    return out.reshape(c.B, c.S, c.D)


def kernel(**inputs):
    cfg = full_cfg()
    in_maps = host_layout(cfg, **inputs)
    with allow_ici_all_to_all():
        nc = build_nc(cfg)
        res = run_bass_kernel_spmd(nc, in_maps, core_ids=list(range(NCORES)))
    return assemble(cfg, res.results)
